# version 13
# baseline (speedup 1.0000x reference)
import os
import numpy as np
import ml_dtypes
from contextlib import ExitStack

KDBG = os.environ.get("KDBG", "") == "1"

import concourse.bass as bass
import concourse.mybir as mybir
from concourse.bass_utils import run_bass_kernel_spmd

F32 = mybir.dt.float32
BF16 = mybir.dt.bfloat16
U8 = mybir.dt.uint8
FP8 = mybir.dt.float8e4
OP = mybir.AluOpType
AX = mybir.AxisListType.X
AF = mybir.ActivationFunctionType

B3_22 = 12582912.0
B3_51 = 6755399441055744.0

T = 1024
D = 2048
IA = 8192
IF = 6144
NM = 8
EPS = 1e-5
R2048 = 1.0 / 2048.0
R8192 = 1.0 / 8192.0
RWG = 1.0 / (2 * IA * D)
RWD = 1.0 / (D * IA)

ENGINES = ("sync", "scalar", "vector", "tensor", "gpsimd")


class Arena:
    def __init__(self, tensor, cols, name):
        self.t, self.cols, self.name = tensor, cols, name
        self.off, self.marks, self.hi = 0, [], 0

    def alloc(self, n):
        assert self.off + n <= self.cols, \
            f"arena {self.name}: {self.off}+{n} > {self.cols}"
        ap = self.t[:, self.off:self.off + n]
        self.off += n
        self.hi = max(self.hi, self.off)
        return ap

    def push(self):
        self.marks.append(self.off)

    def pop(self):
        self.off = self.marks.pop()


class Sched:
    def __init__(self, sems):
        self.ops = {e: [] for e in ENGINES}
        self.cnt = {n: 0 for n in sems}
        self.sem = dict(sems)

    def op(self, eng, fn, inc=None, inc_n=1):
        if inc is None:
            self.ops[eng].append(("op", fn, None, 0))
            return None
        self.cnt[inc] += inc_n
        self.ops[eng].append(("op", fn, self.sem[inc], inc_n))
        return self.cnt[inc]

    def dma(self, eng, q, fn):
        self.cnt[q] += 16
        self.ops[eng].append(("op", fn, self.sem[q], 16))
        return self.cnt[q]

    def wait(self, eng, q, val):
        if val:
            self.ops[eng].append(("wait", self.sem[q], val))

    def wait_tok(self, eng, tok):
        if tok is not None:
            self.wait(eng, tok[0], tok[1])

    def emit(self, eng, h):
        for it in self.ops[eng]:
            if it[0] == "wait":
                h.wait_ge(it[1], it[2])
            else:
                _, fn, sem, n = it
                inst = fn(h)
                if sem is not None:
                    inst.then_inc(sem, n)


class Buf:
    def __init__(self, ap):
        self.ap = ap
        self.frees = []

    def wait_free(self, S, eng):
        for tok in self.frees:
            S.wait(eng, tok[0], tok[1])
        self.frees = []

    def used(self, q, tok):
        self.frees.append((q, tok))


def build_nc():
    nc = bass.Bass("TRN2", target_bir_lowering=False,
                   detect_race_conditions=False)

    x_h = nc.dram_tensor("x", [T, D], F32, kind="ExternalInput")
    wg_h = nc.dram_tensor("wg", [2 * IA, D], F32, kind="ExternalInput")
    gg_h = nc.dram_tensor("gg", [D], F32, kind="ExternalInput")
    wd_h = nc.dram_tensor("wd", [D, IA], F32, kind="ExternalInput")
    gd_h = nc.dram_tensor("gd", [IA], F32, kind="ExternalInput")
    fgw_h = nc.dram_tensor("fgw", [D, IF], F32, kind="ExternalInput")
    fuw_h = nc.dram_tensor("fuw", [D, IF], F32, kind="ExternalInput")
    fdw_h = nc.dram_tensor("fdw", [IF, D], F32, kind="ExternalInput")
    fgs_h = nc.dram_tensor("fgs", [IF], F32, kind="ExternalInput")
    fus_h = nc.dram_tensor("fus", [IF], F32, kind="ExternalInput")
    fds_h = nc.dram_tensor("fds", [D], F32, kind="ExternalInput")
    id_h = nc.dram_tensor("ident", [128, 128], BF16, kind="ExternalInput")
    out_h = nc.dram_tensor("out", [T, D], F32, kind="ExternalOutput")
    kind_dbg = dict(kind="ExternalOutput") if KDBG else {}
    t1_h = nc.dram_tensor("t1scr", [NM, 16, 128, 512], F32, **kind_dbg)
    g_h = nc.dram_tensor("gscr", [NM, 128, D], F32, **kind_dbg)
    if KDBG:
        dbg_h = nc.dram_tensor("dbgout", [128, 64], F32, kind="ExternalOutput")
        dbg2_h = nc.dram_tensor("dbgout2", [128, 384], F32, kind="ExternalOutput")

    st = ExitStack()
    with st:
        f32_t = st.enter_context(nc.sbuf_tensor("AF32", [128, 22528], F32))
        b16_t = st.enter_context(nc.sbuf_tensor("AB16", [128, 39040], BF16))
        u8_t = st.enter_context(nc.sbuf_tensor("AU8", [128, 40960], U8))
        fa = Arena(f32_t, 22528, "f32")
        ba = Arena(b16_t, 39040, "b16")
        ua = Arena(u8_t, 40960, "u8")

        psA = st.enter_context(nc.psum_tensor("psA", [128, 2048], F32))
        psT1 = st.enter_context(nc.psum_tensor("psT1", [128, 1024], BF16))
        psT2 = st.enter_context(nc.psum_tensor("psT2", [128, 1024], BF16))
        psB = st.enter_context(nc.psum_tensor("psB", [128, 512], F32))

        sems = {}
        SEM_NAMES = ("sv", "sa", "st", "qsu", "qx0", "qx1", "qst0", "qst1",
                     "qg0", "qg1", "qd0", "qd1", "qgs0", "qgs1",
                     "qsp0", "qsp1", "qsp2", "qt0", "qt1", "qt2",
                     "qw0", "qw1", "qo0", "qo1")
        DMA_SEMS = tuple(n_ for n_ in SEM_NAMES if n_.startswith("q"))
        for n_ in SEM_NAMES:
            sems[n_] = st.enter_context(nc.semaphore("S_" + n_))
        S = Sched(sems)

        gg_rep = fa.alloc(D)
        fds_rep = fa.alloc(D)
        fgs_fm = fa.alloc(48)
        fus_fm = fa.alloc(48)
        mgc = fa.alloc(1)
        nmgc = fa.alloc(1)
        dsc = fa.alloc(1)
        eps_c = fa.alloc(1)
        ones_c = fa.alloc(1)
        onesr = fa.alloc(128)
        t_a = fa.alloc(1)
        t_b = fa.alloc(1)
        t_c = fa.alloc(1)
        sx = fa.alloc(NM)
        sxx = fa.alloc(NM)
        nmean = fa.alloc(NM)
        rstd = fa.alloc(NM)
        sq1 = fa.alloc(NM)
        mx1 = fa.alloc(NM)
        cm1 = fa.alloc(NM)
        invs1 = fa.alloc(NM)
        c1y = fa.alloc(NM)
        sq2 = fa.alloc(NM * 16)
        am2 = fa.alloc(NM * 16)
        cm2 = fa.alloc(NM)
        c2y = fa.alloc(NM)
        wsums = fa.alloc(192)
        wsums2 = fa.alloc(192)
        invsw1 = fa.alloc(1)
        invsw2 = fa.alloc(1)
        sw1 = fa.alloc(1)
        sw2 = fa.alloc(1)
        invs1b = fa.alloc(NM)
        rs1dbg = fa.alloc(NM)
        ms1dbg = fa.alloc(NM)
        ident = ba.alloc(128)
        xq_fm = ba.alloc(16 * 1024)

        def vchk(fn):
            c = S.op("vector", fn, inc="sv")
            S.wait("vector", "sv", c)
            return c

        def achk(fn):
            c = S.op("scalar", fn, inc="sa")
            S.wait("scalar", "sa", c)
            return c

        psA_free = [[] for _ in range(4)]
        psT_free = {0: [], 1: []}
        psT = {0: psT1, 1: psT2}
        psB_free = []

        def psA_wait(slots):
            for sI in slots:
                for tok in psA_free[sI]:
                    S.wait("tensor", tok[0], tok[1])
                psA_free[sI] = []

        def psT_wait(b):
            for tok in psT_free[b]:
                S.wait("tensor", tok[0], tok[1])
            psT_free[b] = []

        def transpose_group(src_bf16, n_k, dest_fn, clip=None, extra_wait=None,
                            dest_wait=None):
            tokv = None
            if dest_wait is not None:
                if isinstance(dest_wait, Buf):
                    dest_wait.wait_free(S, "vector")
                else:
                    S.wait_tok("vector", dest_wait)
            for g2 in range((n_k + 7) // 8):
                k0 = g2 * 8
                nk = min(8, n_k - k0)
                bsel = g2 % 2
                psT_wait(bsel)
                if extra_wait is not None:
                    S.wait_tok("tensor", extra_wait)
                    extra_wait = None
                for kk in range(nk):
                    tokt = S.op(
                        "tensor",
                        (lambda s_, k_, b_, k0_: lambda h: h.transpose(
                            out=psT[b_][:, k_ * 128:(k_ + 1) * 128],
                            in_=s_[:, (k0_ + k_) * 128:(k0_ + k_ + 1) * 128],
                            identity=ident[:]))(src_bf16, kk, bsel, k0),
                        inc="st" if kk == nk - 1 else None)
                S.wait("vector", "st", tokt)
                dst = dest_fn(k0, nk)
                srcp = psT[bsel][:, 0:nk * 128].rearrange(
                    "p (k c) -> p k c", c=128)
                if clip is None:
                    tokv = S.op("vector",
                                (lambda d_, s_: lambda h: h.tensor_copy(
                                    out=d_, in_=s_))(dst, srcp), inc="sv")
                else:
                    lo_, hi_ = clip
                    tokv = S.op("vector",
                                (lambda d_, s_: lambda h: h.tensor_scalar(
                                    out=d_, in0=s_, scalar1=hi_, scalar2=lo_,
                                    op0=OP.min, op1=OP.max))(dst, srcp),
                                inc="sv")
                psT_free[bsel].append(("sv", tokv))
            return tokv

        def phase_fence(engs=("sync", "scalar", "vector", "gpsimd")):
            for e in engs:
                for q in ("sv", "sa", "st"):
                    S.wait(e, q, S.cnt[q])
                for q in DMA_SEMS:
                    S.wait(e, q, S.cnt[q])

        S.dma("sync", "qsu", lambda h: h.dma_start(out=ident[:], in_=id_h[:]))
        S.dma("sync", "qsu", lambda h: h.dma_start(
            out=gg_rep, in_=gg_h.ap().partition_broadcast(128)))
        S.dma("sync", "qsu", lambda h: h.dma_start(
            out=fds_rep, in_=fds_h.ap().partition_broadcast(128)))
        S.dma("sync", "qsu", lambda h: h.dma_start(
            out=fgs_fm, in_=bass.AP(fgs_h, 0, [[1, 128], [128, 48]]),
            allow_slow_non_contiguous=True))
        S.dma("sync", "qsu", lambda h: h.dma_start(
            out=fus_fm, in_=bass.AP(fus_h, 0, [[1, 128], [128, 48]]),
            allow_slow_non_contiguous=True))

        S.op("vector", lambda h: h.memset(dsc, 1.5))
        S.op("vector", lambda h: h.memset(eps_c, EPS))
        S.op("vector", lambda h: h.memset(ones_c, 1.0))
        S.op("vector", lambda h: h.memset(onesr[0:1, :], 1.0))
        vchk(lambda h: h.tensor_scalar(out=dsc, in0=dsc, scalar1=B3_51,
                                       scalar2=-B3_51, op0=OP.add, op1=OP.add))
        vchk(lambda h: h.tensor_scalar(out=dsc, in0=dsc, scalar1=0.5,
                                       scalar2=None, op0=OP.mult))
        vchk(lambda h: h.tensor_scalar(out=mgc, in0=dsc, scalar1=-B3_22,
                                       scalar2=B3_22, op0=OP.mult, op1=OP.add))
        vchk(lambda h: h.scalar_tensor_tensor(out=mgc, in0=dsc, scalar=B3_51,
                                              op0=OP.mult, in1=mgc, op1=OP.add))
        vchk(lambda h: h.tensor_scalar(out=nmgc, in0=mgc, scalar1=-1.0,
                                       scalar2=None, op0=OP.mult))

        fa.push()
        ba.push()
        ua.push()
        wst = fa.alloc(2048)
        x_t = Buf(fa.alloc(D))
        xn_t = Buf(fa.alloc(D))
        xr_t = Buf(fa.alloc(D))
        dump = fa.alloc(1024)
        f_acc = [fa.alloc(D) for _ in range(4)]
        sg_t = [Buf(x_t.ap[:, 0:512]), Buf(x_t.ap[:, 512:1024])]
        zu_t = [Buf(x_t.ap[:, 1024:1536]), Buf(x_t.ap[:, 1536:2048])]
        fuse_t = Buf(xr_t.ap[:, 0:512])
        gtmp_ap = xn_t.ap
        g_t = Buf(xr_t.ap)
        hi_t = Buf(ba.alloc(D))
        lo_t = Buf(ba.alloc(D))
        xqn_t = Buf(ba.alloc(D))
        hi_fm = ba.alloc(16 * 512)
        lo_fm = ba.alloc(16 * 512)
        fuh_fm = [hi_t.ap[:, i * 512:(i + 1) * 512] for i in range(4)]
        ful_fm = [lo_t.ap[:, i * 512:(i + 1) * 512] for i in range(4)]
        gate_u8 = [Buf(ua.alloc(16 * 512)) for _ in range(2)]
        up_u8 = [Buf(ua.alloc(16 * 512)) for _ in range(2)]
        down_u8 = Buf(ua.alloc(4 * 2048))

        wstat_srcs = []
        wg2 = wg_h.ap().rearrange("(j p) (q c) -> p j q c", p=128, c=1024)
        for j in range(128):
            for q in range(2):
                wstat_srcs.append((wg2[:, j, q, :], None))
        wd4 = wd_h.ap().rearrange("(j p) (q c) -> p j q c", p=128, c=1024)
        for j in range(16):
            for q in range(8):
                wstat_srcs.append((wd4[:, j, q, :], None))
        wstat_state = {"i": 0, "issued": 0, "toks": {}, "stat_tok": [None, None]}
        NWST = len(wstat_srcs)

        def wstat_issue():
            i = wstat_state["issued"]
            if i >= NWST:
                return
            b = wst[:, (i % 2) * 1024:(i % 2) * 1024 + 1024]
            S.wait_tok("scalar", wstat_state["stat_tok"][i % 2])
            wstat_state["toks"][i] = ("qst%d" % (i % 2), S.dma(
                "scalar", "qst%d" % (i % 2),
                (lambda b_, s_: lambda h: h.dma_start(out=b_, in_=s_))
                (b, wstat_srcs[i][0])))
            wstat_state["issued"] = i + 1

        def pump_wstats(n):
            if wstat_state["issued"] == 0:
                wstat_issue()
            for _ in range(n):
                i = wstat_state["i"]
                if i >= NWST:
                    return
                wstat_issue()
                b = wst[:, (i % 2) * 1024:(i % 2) * 1024 + 1024]
                S.wait_tok("scalar", wstat_state["toks"].pop(i))
                wcol = (wsums if i % 2 == 0 else wsums2)[:, i // 2:i // 2 + 1]
                at = S.op("scalar",
                          (lambda b_, w_: lambda h: h.activation(
                              out=dump, in_=b_, func=AF.Abs,
                              accum_out=w_))(b, wcol), inc="sa")
                wstat_state["stat_tok"][i % 2] = ("sa", at)
                wstat_state["i"] = i + 1

        def cross_part_total(lo, hi_, out_inv, out_sw, scale):
            vchk((lambda: lambda h: h.reduce_sum(
                out=t_a, in_=wsums[:, lo:hi_], axis=AX))())
            vchk((lambda: lambda h: h.reduce_sum(
                out=t_b, in_=wsums2[:, lo:hi_], axis=AX))())
            vr = vchk(lambda h: h.tensor_tensor(out=t_a, in0=t_a, in1=t_b,
                                                op=OP.add))
            for tok in psB_free:
                S.wait("tensor", tok[0], tok[1])
            psB_free.clear()
            S.wait("tensor", "sv", vr)
            tk = S.op("tensor", lambda h: h.matmul(
                out=psB[0:1, 0:1], lhsT=t_a, rhs=ones_c, start=True,
                stop=True), inc="st")
            S.wait("vector", "st", tk)
            vv = vchk(lambda h: h.tensor_copy(out=t_b[0:1, :],
                                              in_=psB[0:1, 0:1]))
            S.wait("tensor", "sv", vv)
            tk = S.op("tensor", lambda h: h.matmul(
                out=psB[:, 0:1], lhsT=onesr[0:1, :], rhs=t_b[0:1, 0:1],
                start=True, stop=True), inc="st")
            S.wait("vector", "st", tk)
            vv = vchk(lambda h: h.tensor_copy(out=t_a, in_=psB[:, 0:1]))
            psB_free.append(("sv", vv))
            vchk((lambda sc_: lambda h: h.tensor_scalar(
                out=t_a, in0=t_a, scalar1=sc_, scalar2=1e-5,
                op0=OP.mult, op1=OP.max))(scale))
            vchk((lambda o_: lambda h: h.tensor_copy(out=o_, in_=t_a))(out_inv))
            vchk((lambda o_: lambda h: h.reciprocal(out=o_, in_=t_a))(out_sw))


        S.wait("tensor", "qsu", 80)
        hilo_reader = {"tok": None}
        mm5_prev = {"tok": None}

        fgw3 = fgw_h.ap().rearrange("(k p) n -> p k n", p=128)
        fuw3 = fuw_h.ap().rearrange("(k p) n -> p k n", p=128)
        fdw3 = fdw_h.ap().rearrange("(s p) n -> p s n", p=128)

        def cast_gu(c, par):
            q = "qg%d" % (c % 2)
            for bufset, src3 in ((gate_u8, fgw3), (up_u8, fuw3)):
                b = bufset[par]
                b.wait_free(S, "gpsimd")
                dst = b.ap.bitcast(FP8).rearrange("p (k n) -> p k n", n=512)
                tok = S.dma("gpsimd", q,
                            (lambda d_, s_: lambda h: h.dma_start(
                                out=d_, in_=s_))(dst,
                                                 src3[:, :, c * 512:(c + 1) * 512]))
            return (q, tok)

        def cast_down(c):
            down_u8.wait_free(S, "gpsimd")
            q = "qd%d" % (c % 2)
            dst = down_u8.ap.bitcast(FP8).rearrange("p (s n) -> p s n", n=2048)
            return (q, S.dma("gpsimd", q,
                             (lambda d_, s_: lambda h: h.dma_start(
                                 out=d_, in_=s_))(dst,
                                                  fdw3[:, 4 * c:4 * c + 4, :])))

        for half in range(2):
            for mloc in range(4):
                m = 4 * half + mloc
                x_t.wait_free(S, "sync")
                qx = "qx%d" % (mloc % 2)
                xtok = (qx, S.dma(
                    "sync", qx,
                    (lambda m_: lambda h: h.dma_start(
                        out=x_t.ap, in_=x_h[m_ * 128:(m_ + 1) * 128, :]))(m)))
                S.wait_tok("scalar", xtok)
                xn_t.wait_free(S, "scalar")
                a1 = S.op("scalar",
                          (lambda m_: lambda h: h.activation(
                              out=xn_t.ap, in_=x_t.ap, func=AF.Copy,
                              accum_out=sx[:, m_:m_ + 1]))(m), inc="sa")
                a2 = S.op("scalar",
                          (lambda m_: lambda h: h.activation(
                              out=xn_t.ap, in_=x_t.ap, func=AF.Square,
                              accum_out=sxx[:, m_:m_ + 1]))(m), inc="sa")
                S.wait("vector", "sa", a2)
                vchk((lambda m_: lambda h: h.tensor_scalar(
                    out=nmean[:, m_:m_ + 1], in0=sx[:, m_:m_ + 1],
                    scalar1=-R2048, scalar2=None, op0=OP.mult))(m))
                vchk((lambda m_: lambda h: h.tensor_scalar(
                    out=t_a, in0=sxx[:, m_:m_ + 1], scalar1=R2048,
                    scalar2=None, op0=OP.mult))(m))
                vchk((lambda m_: lambda h: h.tensor_tensor(
                    out=t_b, in0=nmean[:, m_:m_ + 1], in1=nmean[:, m_:m_ + 1],
                    op=OP.mult))(m))
                vv = vchk(lambda h: h.tensor_tensor(
                    out=t_a, in0=t_a, in1=t_b, op=OP.subtract))
                S.wait("scalar", "sv", vv)
                aa = achk(lambda h: h.activation(
                    out=t_b, in_=t_a, func=AF.Sqrt, bias=eps_c))
                S.wait("vector", "sa", aa)
                vchk((lambda m_: lambda h: h.reciprocal(
                    out=rstd[:, m_:m_ + 1], in_=t_b))(m))
                vxn = S.op("vector",
                           (lambda m_: lambda h: h.tensor_scalar(
                               out=xn_t.ap, in0=x_t.ap,
                               scalar1=nmean[:, m_:m_ + 1],
                               scalar2=rstd[:, m_:m_ + 1],
                               op0=OP.add, op1=OP.mult))(m), inc="sv")
                x_t.used("sv", vxn)
                x_t.used("sa", a2)
                S.wait("scalar", "sv", vxn)
                a3 = S.op("scalar",
                          (lambda m_: lambda h: h.activation(
                              out=x_t.ap, in_=xn_t.ap, func=AF.Square,
                              accum_out=sq1[:, m_:m_ + 1]))(m), inc="sa")
                x_t.used("sa", a3)
                hi_t.wait_free(S, "vector")
                vhi = S.op("vector", lambda h: h.tensor_copy(
                    out=hi_t.ap, in_=xn_t.ap), inc="sv")
                lo_t.wait_free(S, "vector")
                vlo = S.op("vector", lambda h: h.tensor_tensor(
                    out=lo_t.ap, in0=xn_t.ap, in1=hi_t.ap, op=OP.subtract),
                    inc="sv")
                xr_t.wait_free(S, "vector")
                vxr = S.op("vector", lambda h: h.tensor_tensor(
                    out=xr_t.ap, in0=xn_t.ap, in1=gg_rep, op=OP.mult),
                    inc="sv")
                xn_t.used("sv", vxr)
                xn_t.used("sa", a3)
                vchk((lambda m_: lambda h: h.reduce_max(
                    out=mx1[:, m_:m_ + 1], in_=xr_t.ap, axis=AX,
                    apply_absolute_value=True))(m))
                S.wait("vector", "sa", a3)
                vchk((lambda m_: lambda h: h.tensor_scalar(
                    out=t_a, in0=sq1[:, m_:m_ + 1], scalar1=R2048,
                    scalar2=None, op0=OP.mult))(m))
                vms = vchk((lambda m_: lambda h: h.tensor_copy(
                    out=ms1dbg[:, m_:m_ + 1], in_=t_a))(m))
                S.wait("scalar", "sv", vms)
                aa = achk(lambda h: h.activation(
                    out=t_b, in_=t_a, func=AF.Sqrt, bias=eps_c))
                S.wait("vector", "sa", aa)
                vchk(lambda h: h.reciprocal(out=t_a, in_=t_b))
                vchk((lambda m_: lambda h: h.tensor_copy(
                    out=rs1dbg[:, m_:m_ + 1], in_=t_a))(m))
                vchk((lambda m_: lambda h: h.tensor_tensor(
                    out=t_b, in0=t_a, in1=mx1[:, m_:m_ + 1], op=OP.mult))(m))
                vchk(lambda h: h.tensor_scalar(
                    out=t_b, in0=t_b, scalar1=1e-5, scalar2=None, op0=OP.max))
                vchk(lambda h: h.reciprocal(out=t_c, in_=t_b))
                vchk(lambda h: h.tensor_scalar(
                    out=t_c, in0=t_c, scalar1=127.0, scalar2=None,
                    op0=OP.mult))
                vchk((lambda m_: lambda h: h.tensor_tensor(
                    out=cm1[:, m_:m_ + 1], in0=t_a, in1=t_c, op=OP.mult))(m))
                vchk((lambda m_: lambda h: h.reciprocal(
                    out=invs1[:, m_:m_ + 1], in_=t_c))(m))
                vchk((lambda m_: lambda h: h.tensor_copy(
                    out=invs1b[:, m_:m_ + 1], in_=invs1[:, m_:m_ + 1]))(m))
                vu = S.op("vector",
                          (lambda m_: lambda h: h.tensor_scalar(
                              out=xr_t.ap, in0=xr_t.ap,
                              scalar1=cm1[:, m_:m_ + 1], scalar2=mgc,
                              op0=OP.mult, op1=OP.add))(m), inc="sv")
                xqn_t.wait_free(S, "vector")
                vq = S.op("vector", lambda h: h.tensor_scalar(
                    out=xqn_t.ap, in0=xr_t.ap, scalar1=nmgc, scalar2=-128.0,
                    op0=OP.add, op1=OP.max), inc="sv")
                xr_t.used("sv", vq)
                tl = mloc * 128

                def d_hi(k0, nk, _t=tl, _b=hi_fm):
                    return _b.rearrange("p (k c) -> p k c", c=512)[
                        :, k0:k0 + nk, _t:_t + 128]

                def d_lo(k0, nk, _t=tl, _b=lo_fm):
                    return _b.rearrange("p (k c) -> p k c", c=512)[
                        :, k0:k0 + nk, _t:_t + 128]

                def d_xq(k0, nk, _t=m * 128, _b=xq_fm):
                    return _b.rearrange("p (k c) -> p k c", c=1024)[
                        :, k0:k0 + nk, _t:_t + 128]

                dw = hilo_reader["tok"] if mloc == 0 else None
                transpose_group(hi_t.ap, 16, d_hi, extra_wait=("sv", vhi),
                                dest_wait=dw)
                hi_t.used("st", S.cnt["st"])
                transpose_group(lo_t.ap, 16, d_lo, extra_wait=("sv", vlo),
                                dest_wait=dw)
                lo_t.used("st", S.cnt["st"])
                transpose_group(xqn_t.ap, 16, d_xq, clip=(-128.0, 127.0),
                                extra_wait=("sv", vq))
                xqn_t.used("st", S.cnt["st"])
                pump_wstats(12)

            hi3 = hi_fm.rearrange("p (k c) -> p k c", c=512)
            lo3 = lo_fm.rearrange("p (k c) -> p k c", c=512)
            tok_gu = {0: cast_gu(0, 0)}
            tok_dn = {0: cast_down(0)}
            last_p0 = ("sv", S.cnt["sv"])
            for c in range(12):
                par = c % 2
                if c + 1 < 12:
                    tok_gu[c + 1] = cast_gu(c + 1, (c + 1) % 2)
                S.wait_tok("tensor", tok_gu[c])
                S.wait_tok("tensor", tok_dn[c])
                if c == 0:
                    S.wait_tok("tensor", last_p0)
                gate8 = gate_u8[par].ap.bitcast(FP8).rearrange(
                    "p (k n) -> p k n", n=512)
                up8 = up_u8[par].ap.bitcast(FP8).rearrange(
                    "p (k n) -> p k n", n=512)
                down8 = down_u8.ap.bitcast(FP8).rearrange(
                    "p (s n) -> p s n", n=2048)
                if mm5_prev["tok"] is not None:
                    S.wait("vector", "st", mm5_prev["tok"])
                fu_last = None
                for ig in range(4):
                    s0 = (ig % 2) * 2
                    psA_wait([s0, s0 + 1])
                    for mat, w8 in ((0, gate8), (1, up8)):
                        slot = psA[:, (s0 + mat) * 512:(s0 + mat + 1) * 512]
                        nmm = 0
                        for k in range(16):
                            for src_fm in (hi3, lo3):
                                tokmm = S.op(
                                    "tensor",
                                    (lambda sl_, w_, k_, ig_, sf_, st_, sp_:
                                     lambda h: h.matmul(
                                         out=sl_,
                                         lhsT=w_[:, k_, ig_ * 128:(ig_ + 1) * 128],
                                         rhs=sf_[:, k_, :],
                                         start=st_, stop=sp_))(
                                        slot, w8, k, ig, src_fm,
                                        nmm == 0, nmm == 31),
                                    inc="st" if nmm == 31 else None)
                                nmm += 1
                        S.wait("scalar", "st", tokmm)
                        cc = c * 4 + ig
                        dstb = (sg_t if mat == 0 else zu_t)[ig % 2]
                        dstb.wait_free(S, "scalar")
                        fn_ = AF.Silu if mat == 0 else AF.Copy
                        sc_ = (fgs_fm if mat == 0 else fus_fm)[:, cc:cc + 1]
                        at = S.op("scalar",
                                  (lambda sl_, d_, f_, s_: lambda h:
                                   h.activation(out=d_, in_=sl_, func=f_,
                                                scale=s_))(
                                      slot, dstb.ap, fn_, sc_), inc="sa")
                        psA_free[s0 + mat].append(("sa", at))
                    S.wait("vector", "sa", at)
                    fuse_t.wait_free(S, "vector")
                    v1 = S.op("vector",
                              (lambda a_, b_: lambda h: h.tensor_tensor(
                                  out=fuse_t.ap, in0=a_, in1=b_, op=OP.mult))(
                                  sg_t[ig % 2].ap, zu_t[ig % 2].ap), inc="sv")
                    sg_t[ig % 2].used("sv", v1)
                    zu_t[ig % 2].used("sv", v1)
                    v2 = S.op("vector",
                              (lambda ig_: lambda h: h.tensor_copy(
                                  out=fuh_fm[ig_], in_=fuse_t.ap))(ig),
                              inc="sv")
                    v3 = S.op("vector",
                              (lambda ig_: lambda h: h.tensor_tensor(
                                  out=ful_fm[ig_], in0=fuse_t.ap,
                                  in1=fuh_fm[ig_], op=OP.subtract))(ig),
                              inc="sv")
                    fuse_t.used("sv", v3)
                    fu_last = v3
                    pump_wstats(3)
                for mloc in range(4):
                    if mloc == 0:
                        S.wait("tensor", "sv", fu_last)
                    for op_ in range(2):
                        psA_wait([2 * op_, 2 * op_ + 1])
                        for kk in range(8):
                            ig, fm_ = kk // 2, (fuh_fm, ful_fm)[kk % 2]
                            for oq in (2 * op_, 2 * op_ + 1):
                                tokmm = S.op(
                                    "tensor",
                                    (lambda oq_, ig_, f_, ml_, st_, sp_:
                                     lambda h: h.matmul(
                                         out=psA[:, oq_ * 512:(oq_ + 1) * 512],
                                         lhsT=f_[ig_][:, ml_ * 128:(ml_ + 1) * 128],
                                         rhs=down8[:, ig_, oq_ * 512:(oq_ + 1) * 512],
                                         start=st_, stop=sp_))(
                                        oq, ig, fm_, mloc,
                                        kk == 0, kk == 7),
                                    inc="st" if (kk == 7 and oq == 2 * op_ + 1)
                                    else None)
                        S.wait("vector", "st", tokmm)
                        sl_ = slice(op_ * 1024, op_ * 1024 + 1024)
                        if c == 0:
                            va = S.op("vector",
                                      (lambda ml_, s2_: lambda h: h.tensor_copy(
                                          out=f_acc[ml_][:, s2_],
                                          in_=psA[:, s2_]))(mloc, sl_),
                                      inc="sv")
                        else:
                            va = S.op("vector",
                                      (lambda ml_, s2_: lambda h: h.tensor_tensor(
                                          out=f_acc[ml_][:, s2_],
                                          in0=f_acc[ml_][:, s2_],
                                          in1=psA[:, s2_], op=OP.add))(mloc, sl_),
                                      inc="sv")
                        psA_free[2 * op_].append(("sv", va))
                        psA_free[2 * op_ + 1].append(("sv", va))
                gate_u8[par].used("st", S.cnt["st"])
                up_u8[par].used("st", S.cnt["st"])
                down_u8.used("st", S.cnt["st"])
                mm5_prev["tok"] = S.cnt["st"]
                hilo_reader["tok"] = ("st", S.cnt["st"])
                if c + 1 < 12:
                    tok_dn[c + 1] = cast_down(c + 1)
            x_t.used("sv", S.cnt["sv"])
            x_t.used("sa", S.cnt["sa"])
            xr_t.used("sv", S.cnt["sv"])
            if half == 1:
                pump_wstats(400)
                S.wait("vector", "sa", S.cnt["sa"])
                cross_part_total(0, 128, invsw1, sw1, RWG)
                cross_part_total(128, 192, invsw2, sw2, RWD)
                for m2 in range(NM):
                    vchk((lambda m_: lambda h: h.tensor_tensor(
                        out=c1y[:, m_:m_ + 1], in0=invs1[:, m_:m_ + 1],
                        in1=invsw1, op=OP.mult))(m2))
            for mloc in range(4):
                m = 4 * half + mloc
                xn_t.wait_free(S, "vector")
                v1 = S.op("vector",
                          (lambda ml_: lambda h: h.tensor_tensor(
                              out=gtmp_ap, in0=f_acc[ml_], in1=fds_rep,
                              op=OP.mult))(mloc), inc="sv")
                xn_t.used("sv", v1)
                S.wait("scalar", "sv", v1)
                g_t.wait_free(S, "scalar")
                at = S.op("scalar", lambda h: h.activation(
                    out=g_t.ap, in_=gtmp_ap, func=AF.Silu), inc="sa")
                xn_t.used("sa", at)
                S.wait("scalar", "sa", at)
                qn = "qgs%d" % (mloc % 2)
                dt_ = S.dma("scalar", qn,
                            (lambda m_: lambda h: h.dma_start(
                                out=g_h.ap()[m_], in_=g_t.ap))(m))
                g_t.used(qn, dt_)
                xr_t.used(qn, dt_)

        fa.pop()
        ba.pop()
        ua.pop()

        fa.push()
        ba.push()
        ua.push()
        phase_fence()
        S.wait("scalar", "sv", S.cnt["sv"])
        gd_rep = fa.alloc(IA)
        wstage = [Buf(fa.alloc(D)) for _ in range(2)]
        gs_t = [Buf(fa.alloc(512)) for _ in range(2)]
        h_t = [Buf(fa.alloc(512)) for _ in range(2)]
        t1_t = [Buf(fa.alloc(512)) for _ in range(3)]
        dumpA = fa.alloc(512)
        wqn_t = [Buf(ba.alloc(D)) for _ in range(2)]
        wq8 = [Buf(ua.alloc(2 * 16 * 512)) for _ in range(2)]
        wq8_ready = [None, None]

        tok_gd = S.dma("sync", "qsu", lambda h: h.dma_start(
            out=gd_rep, in_=gd_h.ap().partition_broadcast(128)))
        S.wait("vector", "qsu", tok_gd)
        gd3 = gd_rep.rearrange("p (o c) -> p o c", c=512)

        a1_dma = {"i": 0, "toks": {}}

        def a1_issue():
            i = a1_dma["i"]
            if i >= 16 * 8:
                return
            oc, tile = i // 8, i % 8
            row0 = (tile // 4) * IA + oc * 512 + (tile % 4) * 128
            sb = wstage[i % 2]
            sb.wait_free(S, "sync")
            a1_dma["toks"][i] = ("qw%d" % (i % 2), S.dma(
                "sync", "qw%d" % (i % 2),
                (lambda b_, r_: lambda h: h.dma_start(
                    out=b_, in_=wg_h[r_:r_ + 128, :]))(sb.ap, row0)))
            a1_dma["i"] = i + 1

        def a1_prep(oc, tile):
            i = oc * 8 + tile
            a1_issue()
            sb = wstage[i % 2]
            S.wait_tok("vector", a1_dma["toks"].pop(i))
            v1 = S.op("vector",
                      (lambda b_: lambda h: h.tensor_scalar(
                          out=b_, in0=b_, scalar1=sw1, scalar2=mgc,
                          op0=OP.mult, op1=OP.add))(sb.ap), inc="sv")
            wb = wqn_t[i % 2]
            wb.wait_free(S, "vector")
            v2 = S.op("vector",
                      (lambda b_, w_: lambda h: h.tensor_scalar(
                          out=w_, in0=b_, scalar1=nmgc, scalar2=None,
                          op0=OP.add))(sb.ap, wb.ap), inc="sv")
            sb.used("sv", v2)
            par = oc % 2
            w4 = wq8[par].ap.bitcast(FP8).rearrange(
                "p (v k n) -> p v k n", k=16, n=512)
            hv, j = tile // 4, tile % 4

            def d_w(k0, nk, _hv=hv, _j=j, _w4=w4):
                return _w4[:, _hv, k0:k0 + nk, _j * 128:_j * 128 + 128]

            dwk = wq8[par] if tile == 0 else None
            tok = transpose_group(wb.ap, 16, d_w, clip=(-1.0, 1.0),
                                  extra_wait=("sv", v2), dest_wait=dwk)
            wb.used("st", S.cnt["st"])
            wq8_ready[par] = tok

        a1_issue()
        for tile in range(8):
            a1_prep(0, tile)
        xq3 = xq_fm.rearrange("p (k c) -> p k c", c=1024)
        for oc in range(16):
            par = oc % 2
            w4 = wq8[par].ap.bitcast(FP8).rearrange(
                "p (v k n) -> p v k n", k=16, n=512)
            S.wait("tensor", "sv", wq8_ready[par])
            for m in range(NM):
                s0 = (m % 2) * 2
                psA_wait([s0, s0 + 1])
                for k in range(16):
                    for mat in (0, 1):
                        slot = psA[:, (s0 + mat) * 512:(s0 + mat + 1) * 512]
                        tok_y = S.op(
                            "tensor",
                            (lambda sl_, mt_, k_, m_, st_, sp_, w4_: lambda h:
                             h.matmul(out=sl_,
                                      lhsT=xq3[:, k_, m_ * 128:(m_ + 1) * 128],
                                      rhs=w4_[:, mt_, k_, :],
                                      start=st_, stop=sp_))(
                                slot, mat, k, m, k == 0, k == 15, w4),
                            inc="st" if (k == 15 and mat == 1) else None)
                if oc + 1 < 16:
                    a1_prep(oc + 1, m)
                S.wait("scalar", "st", tok_y)
                gb = gs_t[m % 2]
                gb.wait_free(S, "scalar")
                at = S.op("scalar",
                          (lambda sl_, d_, m_: lambda h: h.activation(
                              out=d_, in_=sl_, func=AF.Silu,
                              scale=c1y[:, m_:m_ + 1]))(
                              psA[:, s0 * 512:(s0 + 1) * 512], gb.ap, m),
                          inc="sa")
                psA_free[s0].append(("sa", at))
                S.wait("vector", "sa", at)
                hb = h_t[m % 2]
                hb.wait_free(S, "vector")
                v1 = S.op("vector",
                          (lambda sl_, g_, d_, m_: lambda h:
                           h.scalar_tensor_tensor(
                               out=d_, in0=sl_, scalar=c1y[:, m_:m_ + 1],
                               op0=OP.mult, in1=g_, op1=OP.mult))(
                              psA[:, (s0 + 1) * 512:(s0 + 2) * 512],
                              gb.ap, hb.ap, m), inc="sv")
                psA_free[s0 + 1].append(("sv", v1))
                gb.used("sv", v1)
                S.wait("scalar", "sv", v1)
                a2 = S.op("scalar",
                          (lambda d_, m_, oc_: lambda h: h.activation(
                              out=dumpA, in_=d_, func=AF.Square,
                              accum_out=sq2[:, m_ * 16 + oc_:m_ * 16 + oc_ + 1]))(
                              hb.ap, m, oc), inc="sa")
                tb = t1_t[m % 3]
                tb.wait_free(S, "vector")
                v2 = S.op("vector",
                          (lambda h_, d_, oc_: lambda h: h.tensor_tensor(
                              out=d_, in0=h_, in1=gd3[:, oc_, :], op=OP.mult))(
                              hb.ap, tb.ap, oc), inc="sv")
                hb.used("sv", v2)
                hb.used("sa", a2)
                v3 = S.op("vector",
                          (lambda d_, m_, oc_: lambda h: h.reduce_max(
                              out=am2[:, m_ * 16 + oc_:m_ * 16 + oc_ + 1],
                              in_=d_, axis=AX, apply_absolute_value=True))(
                              tb.ap, m, oc), inc="sv")
                S.wait("scalar", "sv", v3)
                qn = "qsp%d" % (m % 3)
                dt_ = S.dma("scalar", qn,
                            (lambda d_, m_, oc_: lambda h: h.dma_start(
                                out=t1_h.ap()[m_, oc_], in_=d_))(tb.ap, m, oc))
                tb.used(qn, dt_)
            wq8[par].used("st", S.cnt["st"])

        S.wait("vector", "sa", S.cnt["sa"])
        for m in range(NM):
            vchk((lambda m_: lambda h: h.reduce_sum(
                out=t_a, in_=sq2[:, m_ * 16:m_ * 16 + 16], axis=AX))(m))
            vms2 = vchk(lambda h: h.tensor_scalar(
                out=t_a, in0=t_a, scalar1=R8192, scalar2=None, op0=OP.mult))
            S.wait("scalar", "sv", vms2)
            aa = achk(lambda h: h.activation(
                out=t_b, in_=t_a, func=AF.Sqrt, bias=eps_c))
            S.wait("vector", "sa", aa)
            vchk(lambda h: h.reciprocal(out=t_a, in_=t_b))
            vchk((lambda m_: lambda h: h.reduce_max(
                out=t_b, in_=am2[:, m_ * 16:m_ * 16 + 16], axis=AX))(m))
            vchk(lambda h: h.tensor_tensor(out=t_b, in0=t_a, in1=t_b,
                                           op=OP.mult))
            vchk(lambda h: h.tensor_scalar(
                out=t_b, in0=t_b, scalar1=1e-5, scalar2=None, op0=OP.max))
            vchk(lambda h: h.reciprocal(out=t_c, in_=t_b))
            vchk(lambda h: h.tensor_scalar(
                out=t_c, in0=t_c, scalar1=127.0, scalar2=None, op0=OP.mult))
            vchk((lambda m_: lambda h: h.tensor_tensor(
                out=cm2[:, m_:m_ + 1], in0=t_a, in1=t_c, op=OP.mult))(m))
            vchk(lambda h: h.reciprocal(out=t_b, in_=t_c))
            vchk((lambda m_: lambda h: h.tensor_tensor(
                out=c2y[:, m_:m_ + 1], in0=t_b, in1=invsw2, op=OP.mult))(m))
        fa.pop()
        ba.pop()
        ua.pop()

        fa.push()
        ba.push()
        ua.push()
        phase_fence()
        S.wait("scalar", "sv", S.cnt["sv"])
        s_acc = [fa.alloc(D) for _ in range(4)]
        w2stage = [Buf(fa.alloc(1024)) for _ in range(2)]
        t1r_t = [Buf(fa.alloc(1024)) for _ in range(3)]
        xf_t = Buf(fa.alloc(D))
        gf_t = Buf(fa.alloc(D))
        wq2n_t = [Buf(ba.alloc(1024)) for _ in range(2)]
        hq_fm2 = [Buf(ba.alloc(1024)) for _ in range(2)]
        hqn_t = Buf(ba.alloc(1024))
        wq28 = [Buf(ua.alloc(8 * 2048)) for _ in range(2)]
        wq28_ready = [None, None]

        wd_rows = wd_h.ap().rearrange("(j p) n -> p j n", p=128)
        a2_dma = {"i": 0, "toks": {}}

        def a2_issue():
            i = a2_dma["i"]
            if i >= 8 * 16:
                return
            ic, tile = i // 16, i % 16
            sb = w2stage[i % 2]
            sb.wait_free(S, "sync")
            a2_dma["toks"][i] = ("qw%d" % (i % 2), S.dma(
                "sync", "qw%d" % (i % 2),
                (lambda b_, t_, ic_: lambda h: h.dma_start(
                    out=b_, in_=wd_rows[:, t_, ic_ * 1024:(ic_ + 1) * 1024]))(
                    sb.ap, tile, ic)))
            a2_dma["i"] = i + 1

        def a2_prep(half, ic, tile):
            i = ic * 16 + tile
            a2_issue()
            sb = w2stage[i % 2]
            S.wait_tok("scalar", a2_dma["toks"].pop(i))
            s1 = S.op("scalar",
                      (lambda b_: lambda h: h.activation(
                          out=b_, in_=b_, func=AF.Relu,
                          scale=sw2, bias=mgc))(sb.ap), inc="sa")
            wb = wq2n_t[i % 2]
            wb.wait_free(S, "vector")
            S.wait("vector", "sa", s1)
            v2 = S.op("vector",
                      (lambda b_, w_: lambda h: h.tensor_scalar(
                          out=w_, in0=b_, scalar1=nmgc, scalar2=None,
                          op0=OP.add))(sb.ap, wb.ap), inc="sv")
            sb.used("sv", v2)
            par = ic % 2
            w3 = wq28[par].ap.bitcast(FP8).rearrange("p (s n) -> p s n", n=2048)

            def d_w2(k0, nk, _j=tile, _w3=w3):
                return _w3[:, k0:k0 + nk, _j * 128:_j * 128 + 128]

            dwk = wq28[par] if tile == 0 else None
            tok = transpose_group(wb.ap, 8, d_w2, clip=(-1.0, 1.0),
                                  extra_wait=("sv", v2), dest_wait=dwk)
            wb.used("st", S.cnt["st"])
            wq28_ready[par] = tok

        t1_reload = {"toks": {}}

        def a2_reload(half, ic, mloc):
            m = 4 * half + mloc
            tb = t1r_t[mloc % 3]
            tb.wait_free(S, "scalar")
            t1_reload["toks"][mloc] = ("qt%d" % (mloc % 3), S.dma(
                "scalar", "qt%d" % (mloc % 3),
                (lambda b_, m_, ic_: lambda h: h.dma_start(
                    out=b_.rearrange("p (o c) -> p o c", c=512),
                    in_=t1_h.ap()[m_, 2 * ic_:2 * ic_ + 2].rearrange(
                        "o p c -> p o c")))(tb.ap, m, ic)))

        hqn2 = [Buf(ba.alloc(1024)) for _ in range(2)]
        hq_pipe = {}

        def a2_hq_quant(half, ic, mloc):
            m = 4 * half + mloc
            tb = t1r_t[mloc % 3]
            S.wait_tok("vector", t1_reload["toks"].pop(mloc))
            v1 = S.op("vector",
                      (lambda b_, m_: lambda h: h.tensor_scalar(
                          out=b_, in0=b_, scalar1=cm2[:, m_:m_ + 1],
                          scalar2=mgc, op0=OP.mult, op1=OP.add))(
                          tb.ap, m), inc="sv")
            qb = hqn2[mloc % 2]
            qb.wait_free(S, "vector")
            v2 = S.op("vector",
                      (lambda b_, q_: lambda h: h.tensor_scalar(
                          out=q_, in0=b_, scalar1=nmgc,
                          scalar2=-128.0, op0=OP.add, op1=OP.max))(
                          tb.ap, qb.ap), inc="sv")
            tb.used("sv", v2)
            hq_pipe[mloc] = (v2, qb)

        def a2_hq_tp(mloc):
            v2, qb = hq_pipe[mloc]
            hqb = hq_fm2[mloc % 2]

            def d_hq(k0, nk, _b=hqb.ap):
                return _b.rearrange("p (s c) -> p s c", c=128)[
                    :, k0:k0 + nk, :]

            tok = transpose_group(qb.ap, 8, d_hq, clip=(-128.0, 127.0),
                                  extra_wait=("sv", v2), dest_wait=hqb)
            qb.used("st", S.cnt["st"])
            hq_pipe[mloc] = (tok, hqb)

        def a2_sevac(pend):
            if pend["e"] is None:
                return
            pm, ptok = pend["e"]
            pfirst = pend["first"]
            S.wait("vector", "st", ptok)
            for hf in range(2):
                sl_ = slice(hf * 1024, hf * 1024 + 1024)
                if pfirst:
                    va = S.op("vector",
                              (lambda ml_, s2_: lambda h: h.tensor_copy(
                                  out=s_acc[ml_][:, s2_],
                                  in_=psA[:, s2_]))(pm, sl_), inc="sv")
                else:
                    va = S.op("vector",
                              (lambda ml_, s2_: lambda h: h.tensor_tensor(
                                  out=s_acc[ml_][:, s2_],
                                  in0=s_acc[ml_][:, s2_],
                                  in1=psA[:, s2_], op=OP.add))(pm, sl_),
                              inc="sv")
                psA_free[2 * hf].append(("sv", va))
                psA_free[2 * hf + 1].append(("sv", va))
            pend["e"] = None

        for half in range(2):
            a2_dma["i"] = 0
            a2_dma["toks"] = {}
            a2_issue()
            for tile in range(16):
                a2_prep(half, 0, tile)

            def jq(j):
                return "qt%d" % (j % 3)

            def j_reload(j):
                if j >= 32:
                    return
                ic, mloc = j // 4, j % 4
                m = 4 * half + mloc
                tb = t1r_t[j % 3]
                tb.wait_free(S, "scalar")
                t1_reload["toks"][j] = (jq(j), S.dma(
                    "scalar", jq(j),
                    (lambda b_, m_, ic_: lambda h: h.dma_start(
                        out=b_.rearrange("p (o c) -> p o c", c=512),
                        in_=t1_h.ap()[m_, 2 * ic_:2 * ic_ + 2].rearrange(
                            "o p c -> p o c")))(tb.ap, m, ic)))

            def j_quant(j):
                if j >= 32:
                    return
                m = 4 * half + (j % 4)
                tb = t1r_t[j % 3]
                S.wait_tok("scalar", t1_reload["toks"].pop(j))
                s1 = S.op("scalar",
                          (lambda b_, m_: lambda h: h.activation(
                              out=b_, in_=b_, func=AF.Relu,
                              scale=cm2[:, m_:m_ + 1], bias=mgc))(
                              tb.ap, m), inc="sa")
                qb = hqn2[j % 2]
                qb.wait_free(S, "vector")
                S.wait("vector", "sa", s1)
                v2 = S.op("vector",
                          (lambda b_, q_: lambda h: h.tensor_scalar(
                              out=q_, in0=b_, scalar1=nmgc,
                              scalar2=-128.0, op0=OP.add, op1=OP.max))(
                              tb.ap, qb.ap), inc="sv")
                tb.used("sv", v2)
                hq_pipe[j] = (v2, qb)

            def j_tp(j):
                if j >= 32:
                    return
                v2, qb = hq_pipe[j]
                hqb = hq_fm2[j % 2]

                def d_hq(k0, nk, _b=hqb.ap):
                    return _b.rearrange("p (s c) -> p s c", c=128)[
                        :, k0:k0 + nk, :]

                tok = transpose_group(qb.ap, 8, d_hq, clip=(-128.0, 127.0),
                                      extra_wait=("sv", v2), dest_wait=hqb)
                qb.used("st", S.cnt["st"])
                hq_pipe[j] = (tok, hqb)

            j_reload(0)
            j_reload(1)
            j_reload(2)
            j_quant(0)
            j_quant(1)
            j_tp(0)
            for j in range(32):
                ic, mloc = j // 4, j % 4
                par = ic % 2
                w3 = wq28[par].ap.bitcast(FP8).rearrange(
                    "p (s n) -> p s n", n=2048)
                if mloc == 0:
                    S.wait("tensor", "sv", wq28_ready[par])
                j_reload(j + 3)
                j_quant(j + 2)
                j_tp(j + 1)
                hq_tok, hqb = hq_pipe.pop(j)
                S.wait("tensor", "sv", hq_tok)
                hq3 = hqb.ap.rearrange("p (s c) -> p s c", c=128)
                evq = []
                for oq in range(4):
                    psA_wait([oq])
                    for s_ in range(8):
                        tok_b = S.op(
                            "tensor",
                            (lambda oq_, s2_, st_, sp_, h3_, w3_:
                             lambda h: h.matmul(
                                 out=psA[:, oq_ * 512:(oq_ + 1) * 512],
                                 lhsT=h3_[:, s2_, :],
                                 rhs=w3_[:, s2_, oq_ * 512:(oq_ + 1) * 512],
                                 start=st_, stop=sp_))(
                                oq, s_, s_ == 0, s_ == 7, hq3, w3),
                            inc="st" if s_ == 7 else None)
                    evq.append((oq, tok_b))
                    if len(evq) > 1:
                        oq_, tokb_ = evq.pop(0)
                        S.wait("vector", "st", tokb_)
                        sl_ = slice(oq_ * 512, oq_ * 512 + 512)
                        if ic == 0:
                            va = S.op("vector",
                                      (lambda ml_, s2_: lambda h: h.tensor_copy(
                                          out=s_acc[ml_][:, s2_],
                                          in_=psA[:, s2_]))(mloc, sl_),
                                      inc="sv")
                        else:
                            va = S.op("vector",
                                      (lambda ml_, s2_: lambda h:
                                       h.tensor_tensor(
                                           out=s_acc[ml_][:, s2_],
                                           in0=s_acc[ml_][:, s2_],
                                           in1=psA[:, s2_], op=OP.add))(
                                          mloc, sl_), inc="sv")
                        psA_free[oq_].append(("sv", va))
                oq_, tokb_ = evq.pop(0)
                S.wait("vector", "st", tokb_)
                sl_ = slice(oq_ * 512, oq_ * 512 + 512)
                if ic == 0:
                    va = S.op("vector",
                              (lambda ml_, s2_: lambda h: h.tensor_copy(
                                  out=s_acc[ml_][:, s2_],
                                  in_=psA[:, s2_]))(mloc, sl_), inc="sv")
                else:
                    va = S.op("vector",
                              (lambda ml_, s2_: lambda h: h.tensor_tensor(
                                  out=s_acc[ml_][:, s2_],
                                  in0=s_acc[ml_][:, s2_],
                                  in1=psA[:, s2_], op=OP.add))(mloc, sl_),
                              inc="sv")
                psA_free[oq_].append(("sv", va))
                hqb.used("st", S.cnt["st"])
                if ic + 1 < 8:
                    for tt_ in range(2):
                        a2_prep(half, ic + 1, 4 * mloc + tt_)
                    for tt_ in range(2):
                        a2_prep(half, ic + 1, 4 * mloc + 2 + tt_)
                if mloc == 3:
                    wq28[par].used("st", S.cnt["st"])
            for mloc in range(4):
                m = 4 * half + mloc
                xf_t.wait_free(S, "sync")
                qx = "qx%d" % (mloc % 2)
                dx = S.dma("sync", qx,
                           (lambda m_: lambda h: h.dma_start(
                               out=xf_t.ap,
                               in_=x_h[m_ * 128:(m_ + 1) * 128, :]))(m))
                gf_t.wait_free(S, "scalar")
                qg_ = "qgs%d" % (mloc % 2)
                dg = S.dma("scalar", qg_,
                           (lambda m_: lambda h: h.dma_start(
                               out=gf_t.ap, in_=g_h.ap()[m_]))(m))
                S.wait("vector", qg_, dg)
                S.wait("vector", qx, dx)
                v1 = S.op("vector",
                          (lambda ml_, m_: lambda h: h.scalar_tensor_tensor(
                              out=gf_t.ap, in0=s_acc[ml_],
                              scalar=c2y[:, m_:m_ + 1], op0=OP.mult,
                              in1=gf_t.ap, op1=OP.mult))(mloc, m), inc="sv")
                v2 = S.op("vector", lambda h: h.tensor_tensor(
                    out=gf_t.ap, in0=gf_t.ap, in1=xf_t.ap, op=OP.add),
                    inc="sv")
                xf_t.used("sv", v2)
                S.wait("scalar", "sv", v2)
                qo = "qo%d" % (mloc % 2)
                do = S.dma("scalar", qo,
                           (lambda m_: lambda h: h.dma_start(
                               out=out_h[m_ * 128:(m_ + 1) * 128, :],
                               in_=gf_t.ap))(m))
                gf_t.used(qo, do)
        fa.pop()
        ba.pop()
        ua.pop()

        DBG.update(dict(invs1b=invs1b, rs1dbg=rs1dbg, ms1dbg=ms1dbg,
                        nmean=nmean, rstd=rstd, sq1=sq1, mx1=mx1, cm1=cm1,
                        invs1=invs1, c1y=c1y, sq2=sq2, am2=am2, cm2=cm2,
                        c2y=c2y, invsw1=invsw1, invsw2=invsw2, sw1=sw1,
                        sw2=sw2, mgc=mgc, xq_fm=xq_fm, dsc=dsc,
                        wsums=wsums, wsums2=wsums2))
        if KDBG:
            def _dump(col, ap, n=8):
                S.op("vector", (lambda c_, a_, n_: lambda h: h.tensor_copy(
                    out=dbgt[:, c_:c_ + n_], in_=a_))(col, ap, n))
            dbgt = fa.alloc(64)
            for i, (nm_, ap_, n_) in enumerate([
                    ("c2y", c2y, 8), ("cm2", cm2, 8), ("c1y", c1y, 8),
                    ("cm1", cm1, 8), ("rstd", rstd, 8), ("nmean", nmean, 8)]):
                _dump(8 * i, ap_, n_)
            _dump(48, sw1, 1)
            _dump(49, sw2, 1)
            _dump(50, invsw1, 1)
            _dump(51, invsw2, 1)
            dl = S.op("vector", lambda h: h.tensor_copy(
                out=dbgt[:, 52:53], in_=mgc), inc="sv")
            S.wait("scalar", "sv", dl)
            S.dma("scalar", "qo0", lambda h: h.dma_start(out=dbg_h[:],
                                                          in_=dbgt))
            S.dma("scalar", "qo0", lambda h: h.dma_start(
                out=dbg2_h[:, 0:192], in_=wsums))
            S.dma("scalar", "qo0", lambda h: h.dma_start(
                out=dbg2_h[:, 192:384], in_=wsums2))
        for q in DMA_SEMS:
            if S.cnt[q]:
                eng = ("gpsimd" if q.startswith(("qg", "qd")) else
                       "sync" if q in ("qsu", "qx0", "qx1", "qw0", "qw1")
                       else "scalar")
                S.wait(eng, q, S.cnt[q])

        blk = st.enter_context(nc.Block())

        @blk.sync
        def _(h):
            S.emit("sync", h)

        @blk.scalar
        def _(h):
            S.emit("scalar", h)

        @blk.vector
        def _(h):
            S.emit("vector", h)

        @blk.tensor
        def _(h):
            S.emit("tensor", h)

        @blk.gpsimd
        def _(h):
            S.emit("gpsimd", h)

    return nc


DBG = {}


_NC_CACHE = {}


def kernel(L0_attn_x, L0_attn_w_gate, L0_attn_g_gate, L0_attn_w_down,
           L0_attn_g_down, L0_ffn_gate_w, L0_ffn_up_w, L0_ffn_down_w,
           L0_ffn_gate_scale, L0_ffn_up_scale, L0_ffn_down_scale):
    if "nc" not in _NC_CACHE:
        _NC_CACHE["nc"] = build_nc()
    nc = _NC_CACHE["nc"]

    x = np.ascontiguousarray(np.asarray(L0_attn_x, dtype=np.float32)
                             .reshape(8 * T, D))
    common = {
        "wg": np.ascontiguousarray(L0_attn_w_gate, dtype=np.float32),
        "gg": np.ascontiguousarray(L0_attn_g_gate, dtype=np.float32),
        "wd": np.ascontiguousarray(L0_attn_w_down, dtype=np.float32),
        "gd": np.ascontiguousarray(L0_attn_g_down, dtype=np.float32),
        "fgw": np.ascontiguousarray(L0_ffn_gate_w, dtype=np.float32),
        "fuw": np.ascontiguousarray(L0_ffn_up_w, dtype=np.float32),
        "fdw": np.ascontiguousarray(L0_ffn_down_w, dtype=np.float32),
        "fgs": np.ascontiguousarray(L0_ffn_gate_scale, dtype=np.float32),
        "fus": np.ascontiguousarray(L0_ffn_up_scale, dtype=np.float32),
        "fds": np.ascontiguousarray(L0_ffn_down_scale, dtype=np.float32),
        "ident": np.eye(128, dtype=ml_dtypes.bfloat16),
    }
    in_maps = []
    for c in range(8):
        mm = dict(common)
        mm["x"] = np.ascontiguousarray(x[c * T:(c + 1) * T])
        in_maps.append(mm)
    res = run_bass_kernel_spmd(nc, in_maps, core_ids=list(range(8)))
    out = np.concatenate([np.asarray(r["out"]) for r in res.results], axis=0)
    return out.reshape(4, 2048, 2048).astype(np.float32, copy=False)


# version 15 (final)
# speedup vs baseline: 1.0305x; 1.0305x over previous
import os
import numpy as np
import ml_dtypes
from contextlib import ExitStack

KDBG = os.environ.get("KDBG", "") == "1"

import concourse.bass as bass
import concourse.mybir as mybir
from concourse.bass_utils import run_bass_kernel_spmd

F32 = mybir.dt.float32
BF16 = mybir.dt.bfloat16
U8 = mybir.dt.uint8
FP8 = mybir.dt.float8e4
OP = mybir.AluOpType
AX = mybir.AxisListType.X
AF = mybir.ActivationFunctionType

B3_22 = 12582912.0
B3_51 = 6755399441055744.0

T = 1024
D = 2048
IA = 8192
IF = 6144
NM = 8
EPS = 1e-5
R2048 = 1.0 / 2048.0
R8192 = 1.0 / 8192.0
RWG = 1.0 / (2 * IA * D)
RWD = 1.0 / (D * IA)

ENGINES = ("sync", "scalar", "vector", "tensor", "gpsimd")


class Arena:
    def __init__(self, tensor, cols, name):
        self.t, self.cols, self.name = tensor, cols, name
        self.off, self.marks, self.hi = 0, [], 0

    def alloc(self, n):
        assert self.off + n <= self.cols, \
            f"arena {self.name}: {self.off}+{n} > {self.cols}"
        ap = self.t[:, self.off:self.off + n]
        self.off += n
        self.hi = max(self.hi, self.off)
        return ap

    def push(self):
        self.marks.append(self.off)

    def pop(self):
        self.off = self.marks.pop()


class Sched:
    def __init__(self, sems):
        self.ops = {e: [] for e in ENGINES}
        self.cnt = {n: 0 for n in sems}
        self.sem = dict(sems)

    def op(self, eng, fn, inc=None, inc_n=1):
        if inc is None:
            self.ops[eng].append(("op", fn, None, 0))
            return None
        self.cnt[inc] += inc_n
        self.ops[eng].append(("op", fn, self.sem[inc], inc_n))
        return self.cnt[inc]

    def dma(self, eng, q, fn):
        self.cnt[q] += 16
        self.ops[eng].append(("op", fn, self.sem[q], 16))
        return self.cnt[q]

    def wait(self, eng, q, val):
        if val:
            self.ops[eng].append(("wait", self.sem[q], val))

    def wait_tok(self, eng, tok):
        if tok is not None:
            self.wait(eng, tok[0], tok[1])

    def emit(self, eng, h):
        for it in self.ops[eng]:
            if it[0] == "wait":
                h.wait_ge(it[1], it[2])
            else:
                _, fn, sem, n = it
                inst = fn(h)
                if sem is not None:
                    inst.then_inc(sem, n)


class Buf:
    def __init__(self, ap):
        self.ap = ap
        self.frees = []

    def wait_free(self, S, eng):
        for tok in self.frees:
            S.wait(eng, tok[0], tok[1])
        self.frees = []

    def used(self, q, tok):
        self.frees.append((q, tok))


def build_nc():
    nc = bass.Bass("TRN2", target_bir_lowering=False,
                   detect_race_conditions=False)

    x_h = nc.dram_tensor("x", [T, D], F32, kind="ExternalInput")
    wg_h = nc.dram_tensor("wg", [2 * IA, D], F32, kind="ExternalInput")
    gg_h = nc.dram_tensor("gg", [D], F32, kind="ExternalInput")
    wd_h = nc.dram_tensor("wd", [D, IA], F32, kind="ExternalInput")
    gd_h = nc.dram_tensor("gd", [IA], F32, kind="ExternalInput")
    fgw_h = nc.dram_tensor("fgw", [D, IF], F32, kind="ExternalInput")
    fuw_h = nc.dram_tensor("fuw", [D, IF], F32, kind="ExternalInput")
    fdw_h = nc.dram_tensor("fdw", [IF, D], F32, kind="ExternalInput")
    fgs_h = nc.dram_tensor("fgs", [IF], F32, kind="ExternalInput")
    fus_h = nc.dram_tensor("fus", [IF], F32, kind="ExternalInput")
    fds_h = nc.dram_tensor("fds", [D], F32, kind="ExternalInput")
    id_h = nc.dram_tensor("ident", [128, 128], BF16, kind="ExternalInput")
    out_h = nc.dram_tensor("out", [T, D], F32, kind="ExternalOutput")
    kind_dbg = dict(kind="ExternalOutput") if KDBG else {}
    t1_h = nc.dram_tensor("t1scr", [NM, 16, 128, 512], F32, **kind_dbg)
    g_h = nc.dram_tensor("gscr", [NM, 128, D], F32, **kind_dbg)
    if KDBG:
        dbg_h = nc.dram_tensor("dbgout", [128, 64], F32, kind="ExternalOutput")
        dbg2_h = nc.dram_tensor("dbgout2", [128, 384], F32, kind="ExternalOutput")

    st = ExitStack()
    with st:
        f32_t = st.enter_context(nc.sbuf_tensor("AF32", [128, 22528], F32))
        b16_t = st.enter_context(nc.sbuf_tensor("AB16", [128, 39040], BF16))
        u8_t = st.enter_context(nc.sbuf_tensor("AU8", [128, 40960], U8))
        fa = Arena(f32_t, 22528, "f32")
        ba = Arena(b16_t, 39040, "b16")
        ua = Arena(u8_t, 40960, "u8")

        psA = st.enter_context(nc.psum_tensor("psA", [128, 2048], F32))
        psT1 = st.enter_context(nc.psum_tensor("psT1", [128, 1024], BF16))
        psT2 = st.enter_context(nc.psum_tensor("psT2", [128, 1024], BF16))
        psB = st.enter_context(nc.psum_tensor("psB", [128, 512], F32))

        sems = {}
        SEM_NAMES = ("sv", "sa", "st", "qsu", "qx0", "qx1", "qst0", "qst1",
                     "qg0", "qg1", "qd0", "qd1", "qgs0", "qgs1",
                     "qsp0", "qsp1", "qsp2", "qt0", "qt1", "qt2",
                     "qw0", "qw1", "qo0", "qo1")
        DMA_SEMS = tuple(n_ for n_ in SEM_NAMES if n_.startswith("q"))
        for n_ in SEM_NAMES:
            sems[n_] = st.enter_context(nc.semaphore("S_" + n_))
        S = Sched(sems)

        gg_rep = fa.alloc(D)
        fds_rep = fa.alloc(D)
        fgs_fm = fa.alloc(48)
        fus_fm = fa.alloc(48)
        mgc = fa.alloc(1)
        nmgc = fa.alloc(1)
        dsc = fa.alloc(1)
        eps_c = fa.alloc(1)
        ones_c = fa.alloc(1)
        onesr = fa.alloc(128)
        t_a = fa.alloc(1)
        t_b = fa.alloc(1)
        t_c = fa.alloc(1)
        sx = fa.alloc(NM)
        sxx = fa.alloc(NM)
        nmean = fa.alloc(NM)
        rstd = fa.alloc(NM)
        sq1 = fa.alloc(NM)
        mx1 = fa.alloc(NM)
        cm1 = fa.alloc(NM)
        invs1 = fa.alloc(NM)
        c1y = fa.alloc(NM)
        sq2 = fa.alloc(NM * 16)
        am2 = fa.alloc(NM * 16)
        cm2 = fa.alloc(NM)
        c2y = fa.alloc(NM)
        wsums = fa.alloc(192)
        wsums2 = fa.alloc(192)
        invsw1 = fa.alloc(1)
        invsw2 = fa.alloc(1)
        sw1 = fa.alloc(1)
        sw2 = fa.alloc(1)
        invs1b = fa.alloc(NM)
        rs1dbg = fa.alloc(NM)
        ms1dbg = fa.alloc(NM)
        ident = ba.alloc(128)
        xq_fm = ba.alloc(16 * 1024)

        def vchk(fn):
            c = S.op("vector", fn, inc="sv")
            S.wait("vector", "sv", c)
            return c

        def achk(fn):
            c = S.op("scalar", fn, inc="sa")
            S.wait("scalar", "sa", c)
            return c

        psA_free = [[] for _ in range(4)]
        psT_free = {0: [], 1: []}
        psT = {0: psT1, 1: psT2}
        psB_free = []

        def psA_wait(slots):
            for sI in slots:
                for tok in psA_free[sI]:
                    S.wait("tensor", tok[0], tok[1])
                psA_free[sI] = []

        def psT_wait(b):
            for tok in psT_free[b]:
                S.wait("tensor", tok[0], tok[1])
            psT_free[b] = []

        def transpose_group(src_bf16, n_k, dest_fn, clip=None, extra_wait=None,
                            dest_wait=None):
            tokv = None
            if dest_wait is not None:
                if isinstance(dest_wait, Buf):
                    dest_wait.wait_free(S, "vector")
                else:
                    S.wait_tok("vector", dest_wait)
            for g2 in range((n_k + 7) // 8):
                k0 = g2 * 8
                nk = min(8, n_k - k0)
                bsel = g2 % 2
                psT_wait(bsel)
                if extra_wait is not None:
                    S.wait_tok("tensor", extra_wait)
                    extra_wait = None
                for kk in range(nk):
                    tokt = S.op(
                        "tensor",
                        (lambda s_, k_, b_, k0_: lambda h: h.transpose(
                            out=psT[b_][:, k_ * 128:(k_ + 1) * 128],
                            in_=s_[:, (k0_ + k_) * 128:(k0_ + k_ + 1) * 128],
                            identity=ident[:]))(src_bf16, kk, bsel, k0),
                        inc="st" if kk == nk - 1 else None)
                S.wait("vector", "st", tokt)
                dst = dest_fn(k0, nk)
                srcp = psT[bsel][:, 0:nk * 128].rearrange(
                    "p (k c) -> p k c", c=128)
                if clip is None:
                    tokv = S.op("vector",
                                (lambda d_, s_: lambda h: h.tensor_copy(
                                    out=d_, in_=s_))(dst, srcp), inc="sv")
                else:
                    lo_, hi_ = clip
                    tokv = S.op("vector",
                                (lambda d_, s_: lambda h: h.tensor_scalar(
                                    out=d_, in0=s_, scalar1=hi_, scalar2=lo_,
                                    op0=OP.min, op1=OP.max))(dst, srcp),
                                inc="sv")
                psT_free[bsel].append(("sv", tokv))
            return tokv

        def phase_fence(engs=("sync", "scalar", "vector", "gpsimd")):
            for e in engs:
                for q in ("sv", "sa", "st"):
                    S.wait(e, q, S.cnt[q])
                for q in DMA_SEMS:
                    S.wait(e, q, S.cnt[q])

        S.dma("sync", "qsu", lambda h: h.dma_start(out=ident[:], in_=id_h[:]))
        S.dma("sync", "qsu", lambda h: h.dma_start(
            out=gg_rep, in_=gg_h.ap().partition_broadcast(128)))
        S.dma("sync", "qsu", lambda h: h.dma_start(
            out=fds_rep, in_=fds_h.ap().partition_broadcast(128)))
        S.dma("sync", "qsu", lambda h: h.dma_start(
            out=fgs_fm, in_=bass.AP(fgs_h, 0, [[1, 128], [128, 48]]),
            allow_slow_non_contiguous=True))
        S.dma("sync", "qsu", lambda h: h.dma_start(
            out=fus_fm, in_=bass.AP(fus_h, 0, [[1, 128], [128, 48]]),
            allow_slow_non_contiguous=True))

        S.op("vector", lambda h: h.memset(dsc, 1.5))
        S.op("vector", lambda h: h.memset(eps_c, EPS))
        S.op("vector", lambda h: h.memset(ones_c, 1.0))
        S.op("vector", lambda h: h.memset(onesr[0:1, :], 1.0))
        vchk(lambda h: h.tensor_scalar(out=dsc, in0=dsc, scalar1=B3_51,
                                       scalar2=-B3_51, op0=OP.add, op1=OP.add))
        vchk(lambda h: h.tensor_scalar(out=dsc, in0=dsc, scalar1=0.5,
                                       scalar2=None, op0=OP.mult))
        vchk(lambda h: h.tensor_scalar(out=mgc, in0=dsc, scalar1=-B3_22,
                                       scalar2=B3_22, op0=OP.mult, op1=OP.add))
        vchk(lambda h: h.scalar_tensor_tensor(out=mgc, in0=dsc, scalar=B3_51,
                                              op0=OP.mult, in1=mgc, op1=OP.add))
        vchk(lambda h: h.tensor_scalar(out=nmgc, in0=mgc, scalar1=-1.0,
                                       scalar2=None, op0=OP.mult))

        fa.push()
        ba.push()
        ua.push()
        wst = fa.alloc(2048)
        x_t = Buf(fa.alloc(D))
        xn_t = Buf(fa.alloc(D))
        xr_t = Buf(fa.alloc(D))
        dump = fa.alloc(1024)
        f_acc = [fa.alloc(D) for _ in range(4)]
        sg_t = [Buf(x_t.ap[:, 0:512]), Buf(x_t.ap[:, 512:1024])]
        zu_t = [Buf(x_t.ap[:, 1024:1536]), Buf(x_t.ap[:, 1536:2048])]
        fuse_t = Buf(xr_t.ap[:, 0:512])
        gtmp_ap = xn_t.ap
        g_t = Buf(xr_t.ap)
        hi_t = Buf(ba.alloc(D))
        lo_t = Buf(ba.alloc(D))
        xqn_t = Buf(ba.alloc(D))
        hi_fm = ba.alloc(16 * 512)
        lo_fm = ba.alloc(16 * 512)
        fuh_fm = [hi_t.ap[:, i * 512:(i + 1) * 512] for i in range(4)]
        ful_fm = [lo_t.ap[:, i * 512:(i + 1) * 512] for i in range(4)]
        gate_u8 = [Buf(ua.alloc(16 * 512)) for _ in range(2)]
        up_u8 = [Buf(ua.alloc(16 * 512)) for _ in range(2)]
        down_u8 = Buf(ua.alloc(4 * 2048))

        wstat_srcs = []
        wg2 = wg_h.ap().rearrange("(j p) (q c) -> p j q c", p=128, c=1024)
        for j in range(128):
            for q in range(2):
                wstat_srcs.append((wg2[:, j, q, :], None))
        wd4 = wd_h.ap().rearrange("(j p) (q c) -> p j q c", p=128, c=1024)
        for j in range(16):
            for q in range(8):
                wstat_srcs.append((wd4[:, j, q, :], None))
        wstat_state = {"i": 0, "issued": 0, "toks": {}, "stat_tok": [None, None]}
        NWST = len(wstat_srcs)

        def wstat_issue():
            i = wstat_state["issued"]
            if i >= NWST:
                return
            b = wst[:, (i % 2) * 1024:(i % 2) * 1024 + 1024]
            S.wait_tok("scalar", wstat_state["stat_tok"][i % 2])
            wstat_state["toks"][i] = ("qst%d" % (i % 2), S.dma(
                "scalar", "qst%d" % (i % 2),
                (lambda b_, s_: lambda h: h.dma_start(out=b_, in_=s_))
                (b, wstat_srcs[i][0])))
            wstat_state["issued"] = i + 1

        def pump_wstats(n):
            if wstat_state["issued"] == 0:
                wstat_issue()
            for _ in range(n):
                i = wstat_state["i"]
                if i >= NWST:
                    return
                wstat_issue()
                b = wst[:, (i % 2) * 1024:(i % 2) * 1024 + 1024]
                S.wait_tok("scalar", wstat_state["toks"].pop(i))
                wcol = (wsums if i % 2 == 0 else wsums2)[:, i // 2:i // 2 + 1]
                at = S.op("scalar",
                          (lambda b_, w_: lambda h: h.activation(
                              out=dump, in_=b_, func=AF.Abs,
                              accum_out=w_))(b, wcol), inc="sa")
                wstat_state["stat_tok"][i % 2] = ("sa", at)
                wstat_state["i"] = i + 1

        def cross_part_total(lo, hi_, out_inv, out_sw, scale):
            vchk((lambda: lambda h: h.reduce_sum(
                out=t_a, in_=wsums[:, lo:hi_], axis=AX))())
            vchk((lambda: lambda h: h.reduce_sum(
                out=t_b, in_=wsums2[:, lo:hi_], axis=AX))())
            vr = vchk(lambda h: h.tensor_tensor(out=t_a, in0=t_a, in1=t_b,
                                                op=OP.add))
            for tok in psB_free:
                S.wait("tensor", tok[0], tok[1])
            psB_free.clear()
            S.wait("tensor", "sv", vr)
            tk = S.op("tensor", lambda h: h.matmul(
                out=psB[0:1, 0:1], lhsT=t_a, rhs=ones_c, start=True,
                stop=True), inc="st")
            S.wait("vector", "st", tk)
            vv = vchk(lambda h: h.tensor_copy(out=t_b[0:1, :],
                                              in_=psB[0:1, 0:1]))
            S.wait("tensor", "sv", vv)
            tk = S.op("tensor", lambda h: h.matmul(
                out=psB[:, 0:1], lhsT=onesr[0:1, :], rhs=t_b[0:1, 0:1],
                start=True, stop=True), inc="st")
            S.wait("vector", "st", tk)
            vv = vchk(lambda h: h.tensor_copy(out=t_a, in_=psB[:, 0:1]))
            psB_free.append(("sv", vv))
            vchk((lambda sc_: lambda h: h.tensor_scalar(
                out=t_a, in0=t_a, scalar1=sc_, scalar2=1e-5,
                op0=OP.mult, op1=OP.max))(scale))
            vchk((lambda o_: lambda h: h.tensor_copy(out=o_, in_=t_a))(out_inv))
            vchk((lambda o_: lambda h: h.reciprocal(out=o_, in_=t_a))(out_sw))


        S.wait("tensor", "qsu", 80)
        hilo_reader = {"tok": None}
        mm5_prev = {"tok": None}

        fgw3 = fgw_h.ap().rearrange("(k p) n -> p k n", p=128)
        fuw3 = fuw_h.ap().rearrange("(k p) n -> p k n", p=128)
        fdw3 = fdw_h.ap().rearrange("(s p) n -> p s n", p=128)

        def cast_gu(c, par):
            q = "qg%d" % (c % 2)
            for bufset, src3 in ((gate_u8, fgw3), (up_u8, fuw3)):
                b = bufset[par]
                b.wait_free(S, "gpsimd")
                dst = b.ap.bitcast(FP8).rearrange("p (k n) -> p k n", n=512)
                tok = S.dma("gpsimd", q,
                            (lambda d_, s_: lambda h: h.dma_start(
                                out=d_, in_=s_))(dst,
                                                 src3[:, :, c * 512:(c + 1) * 512]))
            return (q, tok)

        def cast_down(c):
            down_u8.wait_free(S, "gpsimd")
            q = "qd%d" % (c % 2)
            dst = down_u8.ap.bitcast(FP8).rearrange("p (s n) -> p s n", n=2048)
            return (q, S.dma("gpsimd", q,
                             (lambda d_, s_: lambda h: h.dma_start(
                                 out=d_, in_=s_))(dst,
                                                  fdw3[:, 4 * c:4 * c + 4, :])))

        for half in range(2):
            for mloc in range(4):
                m = 4 * half + mloc
                x_t.wait_free(S, "sync")
                qx = "qx%d" % (mloc % 2)
                xtok = (qx, S.dma(
                    "sync", qx,
                    (lambda m_: lambda h: h.dma_start(
                        out=x_t.ap, in_=x_h[m_ * 128:(m_ + 1) * 128, :]))(m)))
                S.wait_tok("scalar", xtok)
                xn_t.wait_free(S, "scalar")
                a1 = S.op("scalar",
                          (lambda m_: lambda h: h.activation(
                              out=xn_t.ap, in_=x_t.ap, func=AF.Copy,
                              accum_out=sx[:, m_:m_ + 1]))(m), inc="sa")
                a2 = S.op("scalar",
                          (lambda m_: lambda h: h.activation(
                              out=xn_t.ap, in_=x_t.ap, func=AF.Square,
                              accum_out=sxx[:, m_:m_ + 1]))(m), inc="sa")
                S.wait("vector", "sa", a2)
                vchk((lambda m_: lambda h: h.tensor_scalar(
                    out=nmean[:, m_:m_ + 1], in0=sx[:, m_:m_ + 1],
                    scalar1=-R2048, scalar2=None, op0=OP.mult))(m))
                vchk((lambda m_: lambda h: h.tensor_scalar(
                    out=t_a, in0=sxx[:, m_:m_ + 1], scalar1=R2048,
                    scalar2=None, op0=OP.mult))(m))
                vchk((lambda m_: lambda h: h.tensor_tensor(
                    out=t_b, in0=nmean[:, m_:m_ + 1], in1=nmean[:, m_:m_ + 1],
                    op=OP.mult))(m))
                vv = vchk(lambda h: h.tensor_tensor(
                    out=t_a, in0=t_a, in1=t_b, op=OP.subtract))
                S.wait("scalar", "sv", vv)
                aa = achk(lambda h: h.activation(
                    out=t_b, in_=t_a, func=AF.Sqrt, bias=eps_c))
                S.wait("vector", "sa", aa)
                vchk((lambda m_: lambda h: h.reciprocal(
                    out=rstd[:, m_:m_ + 1], in_=t_b))(m))
                vxn = S.op("vector",
                           (lambda m_: lambda h: h.tensor_scalar(
                               out=xn_t.ap, in0=x_t.ap,
                               scalar1=nmean[:, m_:m_ + 1],
                               scalar2=rstd[:, m_:m_ + 1],
                               op0=OP.add, op1=OP.mult))(m), inc="sv")
                x_t.used("sv", vxn)
                x_t.used("sa", a2)
                S.wait("scalar", "sv", vxn)
                a3 = S.op("scalar",
                          (lambda m_: lambda h: h.activation(
                              out=x_t.ap, in_=xn_t.ap, func=AF.Square,
                              accum_out=sq1[:, m_:m_ + 1]))(m), inc="sa")
                x_t.used("sa", a3)
                hi_t.wait_free(S, "vector")
                vhi = S.op("vector", lambda h: h.tensor_copy(
                    out=hi_t.ap, in_=xn_t.ap), inc="sv")
                lo_t.wait_free(S, "vector")
                vlo = S.op("vector", lambda h: h.tensor_tensor(
                    out=lo_t.ap, in0=xn_t.ap, in1=hi_t.ap, op=OP.subtract),
                    inc="sv")
                xr_t.wait_free(S, "vector")
                vxr = S.op("vector", lambda h: h.tensor_tensor(
                    out=xr_t.ap, in0=xn_t.ap, in1=gg_rep, op=OP.mult),
                    inc="sv")
                xn_t.used("sv", vxr)
                xn_t.used("sa", a3)
                vchk((lambda m_: lambda h: h.reduce_max(
                    out=mx1[:, m_:m_ + 1], in_=xr_t.ap, axis=AX,
                    apply_absolute_value=True))(m))
                S.wait("vector", "sa", a3)
                vchk((lambda m_: lambda h: h.tensor_scalar(
                    out=t_a, in0=sq1[:, m_:m_ + 1], scalar1=R2048,
                    scalar2=None, op0=OP.mult))(m))
                vms = vchk((lambda m_: lambda h: h.tensor_copy(
                    out=ms1dbg[:, m_:m_ + 1], in_=t_a))(m))
                S.wait("scalar", "sv", vms)
                aa = achk(lambda h: h.activation(
                    out=t_b, in_=t_a, func=AF.Sqrt, bias=eps_c))
                S.wait("vector", "sa", aa)
                vchk(lambda h: h.reciprocal(out=t_a, in_=t_b))
                vchk((lambda m_: lambda h: h.tensor_copy(
                    out=rs1dbg[:, m_:m_ + 1], in_=t_a))(m))
                vchk((lambda m_: lambda h: h.tensor_tensor(
                    out=t_b, in0=t_a, in1=mx1[:, m_:m_ + 1], op=OP.mult))(m))
                vchk(lambda h: h.tensor_scalar(
                    out=t_b, in0=t_b, scalar1=1e-5, scalar2=None, op0=OP.max))
                vchk(lambda h: h.reciprocal(out=t_c, in_=t_b))
                vchk(lambda h: h.tensor_scalar(
                    out=t_c, in0=t_c, scalar1=127.0, scalar2=None,
                    op0=OP.mult))
                vchk((lambda m_: lambda h: h.tensor_tensor(
                    out=cm1[:, m_:m_ + 1], in0=t_a, in1=t_c, op=OP.mult))(m))
                vchk((lambda m_: lambda h: h.reciprocal(
                    out=invs1[:, m_:m_ + 1], in_=t_c))(m))
                vchk((lambda m_: lambda h: h.tensor_copy(
                    out=invs1b[:, m_:m_ + 1], in_=invs1[:, m_:m_ + 1]))(m))
                vu = S.op("vector",
                          (lambda m_: lambda h: h.tensor_scalar(
                              out=xr_t.ap, in0=xr_t.ap,
                              scalar1=cm1[:, m_:m_ + 1], scalar2=mgc,
                              op0=OP.mult, op1=OP.add))(m), inc="sv")
                xqn_t.wait_free(S, "vector")
                vq = S.op("vector", lambda h: h.tensor_scalar(
                    out=xqn_t.ap, in0=xr_t.ap, scalar1=nmgc, scalar2=-128.0,
                    op0=OP.add, op1=OP.max), inc="sv")
                xr_t.used("sv", vq)
                tl = mloc * 128

                def d_hi(k0, nk, _t=tl, _b=hi_fm):
                    return _b.rearrange("p (k c) -> p k c", c=512)[
                        :, k0:k0 + nk, _t:_t + 128]

                def d_lo(k0, nk, _t=tl, _b=lo_fm):
                    return _b.rearrange("p (k c) -> p k c", c=512)[
                        :, k0:k0 + nk, _t:_t + 128]

                def d_xq(k0, nk, _t=m * 128, _b=xq_fm):
                    return _b.rearrange("p (k c) -> p k c", c=1024)[
                        :, k0:k0 + nk, _t:_t + 128]

                dw = hilo_reader["tok"] if mloc == 0 else None
                transpose_group(hi_t.ap, 16, d_hi, extra_wait=("sv", vhi),
                                dest_wait=dw)
                hi_t.used("st", S.cnt["st"])
                transpose_group(lo_t.ap, 16, d_lo, extra_wait=("sv", vlo),
                                dest_wait=dw)
                lo_t.used("st", S.cnt["st"])
                transpose_group(xqn_t.ap, 16, d_xq, clip=(-128.0, 127.0),
                                extra_wait=("sv", vq))
                xqn_t.used("st", S.cnt["st"])
                pump_wstats(12)

            hi3 = hi_fm.rearrange("p (k c) -> p k c", c=512)
            lo3 = lo_fm.rearrange("p (k c) -> p k c", c=512)
            tok_gu = {0: cast_gu(0, 0)}
            tok_dn = {0: cast_down(0)}
            last_p0 = ("sv", S.cnt["sv"])
            for c in range(12):
                par = c % 2
                if c + 1 < 12:
                    tok_gu[c + 1] = cast_gu(c + 1, (c + 1) % 2)
                S.wait_tok("tensor", tok_gu[c])
                S.wait_tok("tensor", tok_dn[c])
                if c == 0:
                    S.wait_tok("tensor", last_p0)
                gate8 = gate_u8[par].ap.bitcast(FP8).rearrange(
                    "p (k n) -> p k n", n=512)
                up8 = up_u8[par].ap.bitcast(FP8).rearrange(
                    "p (k n) -> p k n", n=512)
                down8 = down_u8.ap.bitcast(FP8).rearrange(
                    "p (s n) -> p s n", n=2048)
                if mm5_prev["tok"] is not None:
                    S.wait("vector", "st", mm5_prev["tok"])
                fu_last = None
                for ig in range(4):
                    s0 = (ig % 2) * 2
                    psA_wait([s0, s0 + 1])
                    for mat, w8 in ((0, gate8), (1, up8)):
                        slot = psA[:, (s0 + mat) * 512:(s0 + mat + 1) * 512]
                        nmm = 0
                        for k in range(16):
                            for src_fm in (hi3, lo3):
                                tokmm = S.op(
                                    "tensor",
                                    (lambda sl_, w_, k_, ig_, sf_, st_, sp_:
                                     lambda h: h.matmul(
                                         out=sl_,
                                         lhsT=w_[:, k_, ig_ * 128:(ig_ + 1) * 128],
                                         rhs=sf_[:, k_, :],
                                         start=st_, stop=sp_))(
                                        slot, w8, k, ig, src_fm,
                                        nmm == 0, nmm == 31),
                                    inc="st" if nmm == 31 else None)
                                nmm += 1
                        S.wait("scalar", "st", tokmm)
                        cc = c * 4 + ig
                        dstb = (sg_t if mat == 0 else zu_t)[ig % 2]
                        dstb.wait_free(S, "scalar")
                        fn_ = AF.Silu if mat == 0 else AF.Copy
                        sc_ = (fgs_fm if mat == 0 else fus_fm)[:, cc:cc + 1]
                        at = S.op("scalar",
                                  (lambda sl_, d_, f_, s_: lambda h:
                                   h.activation(out=d_, in_=sl_, func=f_,
                                                scale=s_))(
                                      slot, dstb.ap, fn_, sc_), inc="sa")
                        psA_free[s0 + mat].append(("sa", at))
                    S.wait("vector", "sa", at)
                    fuse_t.wait_free(S, "vector")
                    v1 = S.op("vector",
                              (lambda a_, b_: lambda h: h.tensor_tensor(
                                  out=fuse_t.ap, in0=a_, in1=b_, op=OP.mult))(
                                  sg_t[ig % 2].ap, zu_t[ig % 2].ap), inc="sv")
                    sg_t[ig % 2].used("sv", v1)
                    zu_t[ig % 2].used("sv", v1)
                    v2 = S.op("vector",
                              (lambda ig_: lambda h: h.tensor_copy(
                                  out=fuh_fm[ig_], in_=fuse_t.ap))(ig),
                              inc="sv")
                    v3 = S.op("vector",
                              (lambda ig_: lambda h: h.tensor_tensor(
                                  out=ful_fm[ig_], in0=fuse_t.ap,
                                  in1=fuh_fm[ig_], op=OP.subtract))(ig),
                              inc="sv")
                    fuse_t.used("sv", v3)
                    fu_last = v3
                    pump_wstats(3)
                for mloc in range(4):
                    if mloc == 0:
                        S.wait("tensor", "sv", fu_last)
                    for op_ in range(2):
                        psA_wait([2 * op_, 2 * op_ + 1])
                        for kk in range(8):
                            ig, fm_ = kk // 2, (fuh_fm, ful_fm)[kk % 2]
                            for oq in (2 * op_, 2 * op_ + 1):
                                tokmm = S.op(
                                    "tensor",
                                    (lambda oq_, ig_, f_, ml_, st_, sp_:
                                     lambda h: h.matmul(
                                         out=psA[:, oq_ * 512:(oq_ + 1) * 512],
                                         lhsT=f_[ig_][:, ml_ * 128:(ml_ + 1) * 128],
                                         rhs=down8[:, ig_, oq_ * 512:(oq_ + 1) * 512],
                                         start=st_, stop=sp_))(
                                        oq, ig, fm_, mloc,
                                        kk == 0, kk == 7),
                                    inc="st" if (kk == 7 and oq == 2 * op_ + 1)
                                    else None)
                        S.wait("vector", "st", tokmm)
                        sl_ = slice(op_ * 1024, op_ * 1024 + 1024)
                        if c == 0:
                            va = S.op("vector",
                                      (lambda ml_, s2_: lambda h: h.tensor_copy(
                                          out=f_acc[ml_][:, s2_],
                                          in_=psA[:, s2_]))(mloc, sl_),
                                      inc="sv")
                        else:
                            va = S.op("vector",
                                      (lambda ml_, s2_: lambda h: h.tensor_tensor(
                                          out=f_acc[ml_][:, s2_],
                                          in0=f_acc[ml_][:, s2_],
                                          in1=psA[:, s2_], op=OP.add))(mloc, sl_),
                                      inc="sv")
                        psA_free[2 * op_].append(("sv", va))
                        psA_free[2 * op_ + 1].append(("sv", va))
                gate_u8[par].used("st", S.cnt["st"])
                up_u8[par].used("st", S.cnt["st"])
                down_u8.used("st", S.cnt["st"])
                mm5_prev["tok"] = S.cnt["st"]
                hilo_reader["tok"] = ("st", S.cnt["st"])
                if c + 1 < 12:
                    tok_dn[c + 1] = cast_down(c + 1)
            x_t.used("sv", S.cnt["sv"])
            x_t.used("sa", S.cnt["sa"])
            xr_t.used("sv", S.cnt["sv"])
            if half == 1:
                pump_wstats(400)
                S.wait("vector", "sa", S.cnt["sa"])
                cross_part_total(0, 128, invsw1, sw1, RWG)
                cross_part_total(128, 192, invsw2, sw2, RWD)
                for m2 in range(NM):
                    vchk((lambda m_: lambda h: h.tensor_tensor(
                        out=c1y[:, m_:m_ + 1], in0=invs1[:, m_:m_ + 1],
                        in1=invsw1, op=OP.mult))(m2))
            for mloc in range(4):
                m = 4 * half + mloc
                xn_t.wait_free(S, "vector")
                v1 = S.op("vector",
                          (lambda ml_: lambda h: h.tensor_tensor(
                              out=gtmp_ap, in0=f_acc[ml_], in1=fds_rep,
                              op=OP.mult))(mloc), inc="sv")
                xn_t.used("sv", v1)
                S.wait("scalar", "sv", v1)
                g_t.wait_free(S, "scalar")
                at = S.op("scalar", lambda h: h.activation(
                    out=g_t.ap, in_=gtmp_ap, func=AF.Silu), inc="sa")
                xn_t.used("sa", at)
                S.wait("scalar", "sa", at)
                qn = "qgs%d" % (mloc % 2)
                dt_ = S.dma("scalar", qn,
                            (lambda m_: lambda h: h.dma_start(
                                out=g_h.ap()[m_], in_=g_t.ap))(m))
                g_t.used(qn, dt_)
                xr_t.used(qn, dt_)

        fa.pop()
        ba.pop()
        ua.pop()

        fa.push()
        ba.push()
        ua.push()
        phase_fence()
        S.wait("scalar", "sv", S.cnt["sv"])
        gd_rep = fa.alloc(IA)
        wstage = [Buf(fa.alloc(D)) for _ in range(2)]
        gs_t = [Buf(fa.alloc(512)) for _ in range(2)]
        h_t = [Buf(fa.alloc(512)) for _ in range(2)]
        t1_t = [Buf(fa.alloc(512)) for _ in range(3)]
        dumpA = fa.alloc(512)
        wqn_t = [Buf(ba.alloc(D)) for _ in range(2)]
        wq8 = [Buf(ua.alloc(2 * 16 * 512)) for _ in range(2)]
        wq8_ready = [None, None]

        tok_gd = S.dma("sync", "qsu", lambda h: h.dma_start(
            out=gd_rep, in_=gd_h.ap().partition_broadcast(128)))
        S.wait("vector", "qsu", tok_gd)
        gd3 = gd_rep.rearrange("p (o c) -> p o c", c=512)

        a1_dma = {"i": 0, "toks": {}}

        def a1_issue():
            i = a1_dma["i"]
            if i >= 16 * 8:
                return
            oc, tile = i // 8, i % 8
            row0 = (tile // 4) * IA + oc * 512 + (tile % 4) * 128
            sb = wstage[i % 2]
            sb.wait_free(S, "sync")
            a1_dma["toks"][i] = ("qw%d" % (i % 2), S.dma(
                "sync", "qw%d" % (i % 2),
                (lambda b_, r_: lambda h: h.dma_start(
                    out=b_, in_=wg_h[r_:r_ + 128, :]))(sb.ap, row0)))
            a1_dma["i"] = i + 1

        def a1_prep(oc, tile):
            i = oc * 8 + tile
            a1_issue()
            sb = wstage[i % 2]
            S.wait_tok("vector", a1_dma["toks"].pop(i))
            v1 = S.op("vector",
                      (lambda b_: lambda h: h.tensor_scalar(
                          out=b_, in0=b_, scalar1=sw1, scalar2=mgc,
                          op0=OP.mult, op1=OP.add))(sb.ap), inc="sv")
            wb = wqn_t[i % 2]
            wb.wait_free(S, "vector")
            v2 = S.op("vector",
                      (lambda b_, w_: lambda h: h.tensor_scalar(
                          out=w_, in0=b_, scalar1=nmgc, scalar2=None,
                          op0=OP.add))(sb.ap, wb.ap), inc="sv")
            sb.used("sv", v2)
            par = oc % 2
            w4 = wq8[par].ap.bitcast(FP8).rearrange(
                "p (v k n) -> p v k n", k=16, n=512)
            hv, j = tile // 4, tile % 4

            def d_w(k0, nk, _hv=hv, _j=j, _w4=w4):
                return _w4[:, _hv, k0:k0 + nk, _j * 128:_j * 128 + 128]

            dwk = wq8[par] if tile == 0 else None
            tok = transpose_group(wb.ap, 16, d_w, clip=(-1.0, 1.0),
                                  extra_wait=("sv", v2), dest_wait=dwk)
            wb.used("st", S.cnt["st"])
            wq8_ready[par] = tok

        a1_issue()
        for tile in range(8):
            a1_prep(0, tile)
        xq3 = xq_fm.rearrange("p (k c) -> p k c", c=1024)
        for oc in range(16):
            par = oc % 2
            w4 = wq8[par].ap.bitcast(FP8).rearrange(
                "p (v k n) -> p v k n", k=16, n=512)
            S.wait("tensor", "sv", wq8_ready[par])
            for m in range(NM):
                s0 = (m % 2) * 2
                psA_wait([s0, s0 + 1])
                for k in range(16):
                    for mat in (0, 1):
                        slot = psA[:, (s0 + mat) * 512:(s0 + mat + 1) * 512]
                        tok_y = S.op(
                            "tensor",
                            (lambda sl_, mt_, k_, m_, st_, sp_, w4_: lambda h:
                             h.matmul(out=sl_,
                                      lhsT=xq3[:, k_, m_ * 128:(m_ + 1) * 128],
                                      rhs=w4_[:, mt_, k_, :],
                                      start=st_, stop=sp_))(
                                slot, mat, k, m, k == 0, k == 15, w4),
                            inc="st" if (k == 15 and mat == 1) else None)
                if oc + 1 < 16:
                    a1_prep(oc + 1, m)
                S.wait("scalar", "st", tok_y)
                gb = gs_t[m % 2]
                gb.wait_free(S, "scalar")
                at = S.op("scalar",
                          (lambda sl_, d_, m_: lambda h: h.activation(
                              out=d_, in_=sl_, func=AF.Silu,
                              scale=c1y[:, m_:m_ + 1]))(
                              psA[:, s0 * 512:(s0 + 1) * 512], gb.ap, m),
                          inc="sa")
                psA_free[s0].append(("sa", at))
                S.wait("vector", "sa", at)
                hb = h_t[m % 2]
                hb.wait_free(S, "vector")
                v1 = S.op("vector",
                          (lambda sl_, g_, d_, m_: lambda h:
                           h.scalar_tensor_tensor(
                               out=d_, in0=sl_, scalar=c1y[:, m_:m_ + 1],
                               op0=OP.mult, in1=g_, op1=OP.mult))(
                              psA[:, (s0 + 1) * 512:(s0 + 2) * 512],
                              gb.ap, hb.ap, m), inc="sv")
                psA_free[s0 + 1].append(("sv", v1))
                gb.used("sv", v1)
                S.wait("scalar", "sv", v1)
                a2 = S.op("scalar",
                          (lambda d_, m_, oc_: lambda h: h.activation(
                              out=dumpA, in_=d_, func=AF.Square,
                              accum_out=sq2[:, m_ * 16 + oc_:m_ * 16 + oc_ + 1]))(
                              hb.ap, m, oc), inc="sa")
                tb = t1_t[m % 3]
                tb.wait_free(S, "vector")
                v2 = S.op("vector",
                          (lambda h_, d_, oc_: lambda h: h.tensor_tensor(
                              out=d_, in0=h_, in1=gd3[:, oc_, :], op=OP.mult))(
                              hb.ap, tb.ap, oc), inc="sv")
                hb.used("sv", v2)
                hb.used("sa", a2)
                v3 = S.op("vector",
                          (lambda d_, m_, oc_: lambda h: h.reduce_max(
                              out=am2[:, m_ * 16 + oc_:m_ * 16 + oc_ + 1],
                              in_=d_, axis=AX, apply_absolute_value=True))(
                              tb.ap, m, oc), inc="sv")
                S.wait("scalar", "sv", v3)
                qn = "qsp%d" % (m % 3)
                dt_ = S.dma("scalar", qn,
                            (lambda d_, m_, oc_: lambda h: h.dma_start(
                                out=t1_h.ap()[m_, oc_], in_=d_))(tb.ap, m, oc))
                tb.used(qn, dt_)
            wq8[par].used("st", S.cnt["st"])

        S.wait("vector", "sa", S.cnt["sa"])
        for m in range(NM):
            vchk((lambda m_: lambda h: h.reduce_sum(
                out=t_a, in_=sq2[:, m_ * 16:m_ * 16 + 16], axis=AX))(m))
            vms2 = vchk(lambda h: h.tensor_scalar(
                out=t_a, in0=t_a, scalar1=R8192, scalar2=None, op0=OP.mult))
            S.wait("scalar", "sv", vms2)
            aa = achk(lambda h: h.activation(
                out=t_b, in_=t_a, func=AF.Sqrt, bias=eps_c))
            S.wait("vector", "sa", aa)
            vchk(lambda h: h.reciprocal(out=t_a, in_=t_b))
            vchk((lambda m_: lambda h: h.reduce_max(
                out=t_b, in_=am2[:, m_ * 16:m_ * 16 + 16], axis=AX))(m))
            vchk(lambda h: h.tensor_tensor(out=t_b, in0=t_a, in1=t_b,
                                           op=OP.mult))
            vchk(lambda h: h.tensor_scalar(
                out=t_b, in0=t_b, scalar1=1e-5, scalar2=None, op0=OP.max))
            vchk(lambda h: h.reciprocal(out=t_c, in_=t_b))
            vchk(lambda h: h.tensor_scalar(
                out=t_c, in0=t_c, scalar1=127.0, scalar2=None, op0=OP.mult))
            vchk((lambda m_: lambda h: h.tensor_tensor(
                out=cm2[:, m_:m_ + 1], in0=t_a, in1=t_c, op=OP.mult))(m))
            vchk(lambda h: h.reciprocal(out=t_b, in_=t_c))
            vchk((lambda m_: lambda h: h.tensor_tensor(
                out=c2y[:, m_:m_ + 1], in0=t_b, in1=invsw2, op=OP.mult))(m))
        fa.pop()
        ba.pop()
        ua.pop()

        fa.push()
        ba.push()
        ua.push()
        phase_fence()
        S.wait("scalar", "sv", S.cnt["sv"])
        s_acc = [fa.alloc(D) for _ in range(4)]
        w2stage = [Buf(fa.alloc(1024)) for _ in range(2)]
        t1r_t = [Buf(fa.alloc(1024)) for _ in range(3)]
        xf_t = Buf(fa.alloc(D))
        gf_t = Buf(fa.alloc(D))
        wq2n_t = [Buf(ba.alloc(1024)) for _ in range(2)]
        hq_fm2 = [Buf(ba.alloc(1024)) for _ in range(2)]
        hqn_t = Buf(ba.alloc(1024))
        wq28 = [Buf(ua.alloc(8 * 2048)) for _ in range(2)]
        wq28_ready = [None, None]

        wd_rows = wd_h.ap().rearrange("(j p) n -> p j n", p=128)
        a2_dma = {"i": 0, "toks": {}}

        def a2_issue():
            i = a2_dma["i"]
            if i >= 8 * 16:
                return
            ic, tile = i // 16, i % 16
            sb = w2stage[i % 2]
            sb.wait_free(S, "sync")
            a2_dma["toks"][i] = ("qw%d" % (i % 2), S.dma(
                "sync", "qw%d" % (i % 2),
                (lambda b_, t_, ic_: lambda h: h.dma_start(
                    out=b_, in_=wd_rows[:, t_, ic_ * 1024:(ic_ + 1) * 1024]))(
                    sb.ap, tile, ic)))
            a2_dma["i"] = i + 1

        def a2_prep(half, ic, tile):
            i = ic * 16 + tile
            a2_issue()
            sb = w2stage[i % 2]
            S.wait_tok("scalar", a2_dma["toks"].pop(i))
            s1 = S.op("scalar",
                      (lambda b_: lambda h: h.activation(
                          out=b_, in_=b_, func=AF.Relu,
                          scale=sw2, bias=mgc))(sb.ap), inc="sa")
            wb = wq2n_t[i % 2]
            wb.wait_free(S, "vector")
            S.wait("vector", "sa", s1)
            v2 = S.op("vector",
                      (lambda b_, w_: lambda h: h.tensor_scalar(
                          out=w_, in0=b_, scalar1=nmgc, scalar2=None,
                          op0=OP.add))(sb.ap, wb.ap), inc="sv")
            sb.used("sv", v2)
            par = ic % 2
            w3 = wq28[par].ap.bitcast(FP8).rearrange("p (s n) -> p s n", n=2048)

            def d_w2(k0, nk, _j=tile, _w3=w3):
                return _w3[:, k0:k0 + nk, _j * 128:_j * 128 + 128]

            dwk = wq28[par] if tile == 0 else None
            tok = transpose_group(wb.ap, 8, d_w2, clip=(-1.0, 1.0),
                                  extra_wait=("sv", v2), dest_wait=dwk)
            wb.used("st", S.cnt["st"])
            wq28_ready[par] = tok

        def a2_combine(half, mloc):
            m = 4 * half + mloc
            xf_t.wait_free(S, "sync")
            qx = "qx%d" % (mloc % 2)
            dx = S.dma("sync", qx,
                       (lambda m_: lambda h: h.dma_start(
                           out=xf_t.ap,
                           in_=x_h[m_ * 128:(m_ + 1) * 128, :]))(m))
            gf_t.wait_free(S, "scalar")
            qg_ = "qgs%d" % (mloc % 2)
            dg = S.dma("scalar", qg_,
                       (lambda m_: lambda h: h.dma_start(
                           out=gf_t.ap, in_=g_h.ap()[m_]))(m))
            S.wait("vector", qg_, dg)
            S.wait("vector", qx, dx)
            v1 = S.op("vector",
                      (lambda ml_, m_: lambda h: h.scalar_tensor_tensor(
                          out=gf_t.ap, in0=s_acc[ml_],
                          scalar=c2y[:, m_:m_ + 1], op0=OP.mult,
                          in1=gf_t.ap, op1=OP.mult))(mloc, m), inc="sv")
            v2 = S.op("vector", lambda h: h.tensor_tensor(
                out=gf_t.ap, in0=gf_t.ap, in1=xf_t.ap, op=OP.add),
                inc="sv")
            xf_t.used("sv", v2)
            S.wait("scalar", "sv", v2)
            qo = "qo%d" % (mloc % 2)
            do = S.dma("scalar", qo,
                       (lambda m_: lambda h: h.dma_start(
                           out=out_h[m_ * 128:(m_ + 1) * 128, :],
                           in_=gf_t.ap))(m))
            gf_t.used(qo, do)

        t1_reload = {"toks": {}}

        def a2_reload(half, ic, mloc):
            m = 4 * half + mloc
            tb = t1r_t[mloc % 3]
            tb.wait_free(S, "scalar")
            t1_reload["toks"][mloc] = ("qt%d" % (mloc % 3), S.dma(
                "scalar", "qt%d" % (mloc % 3),
                (lambda b_, m_, ic_: lambda h: h.dma_start(
                    out=b_.rearrange("p (o c) -> p o c", c=512),
                    in_=t1_h.ap()[m_, 2 * ic_:2 * ic_ + 2].rearrange(
                        "o p c -> p o c")))(tb.ap, m, ic)))

        hqn2 = [Buf(ba.alloc(1024)) for _ in range(2)]
        hq_pipe = {}

        def a2_hq_quant(half, ic, mloc):
            m = 4 * half + mloc
            tb = t1r_t[mloc % 3]
            S.wait_tok("vector", t1_reload["toks"].pop(mloc))
            v1 = S.op("vector",
                      (lambda b_, m_: lambda h: h.tensor_scalar(
                          out=b_, in0=b_, scalar1=cm2[:, m_:m_ + 1],
                          scalar2=mgc, op0=OP.mult, op1=OP.add))(
                          tb.ap, m), inc="sv")
            qb = hqn2[mloc % 2]
            qb.wait_free(S, "vector")
            v2 = S.op("vector",
                      (lambda b_, q_: lambda h: h.tensor_scalar(
                          out=q_, in0=b_, scalar1=nmgc,
                          scalar2=-128.0, op0=OP.add, op1=OP.max))(
                          tb.ap, qb.ap), inc="sv")
            tb.used("sv", v2)
            hq_pipe[mloc] = (v2, qb)

        def a2_hq_tp(mloc):
            v2, qb = hq_pipe[mloc]
            hqb = hq_fm2[mloc % 2]

            def d_hq(k0, nk, _b=hqb.ap):
                return _b.rearrange("p (s c) -> p s c", c=128)[
                    :, k0:k0 + nk, :]

            tok = transpose_group(qb.ap, 8, d_hq, clip=(-128.0, 127.0),
                                  extra_wait=("sv", v2), dest_wait=hqb)
            qb.used("st", S.cnt["st"])
            hq_pipe[mloc] = (tok, hqb)

        def a2_sevac(pend):
            if pend["e"] is None:
                return
            pm, ptok = pend["e"]
            pfirst = pend["first"]
            S.wait("vector", "st", ptok)
            for hf in range(2):
                sl_ = slice(hf * 1024, hf * 1024 + 1024)
                if pfirst:
                    va = S.op("vector",
                              (lambda ml_, s2_: lambda h: h.tensor_copy(
                                  out=s_acc[ml_][:, s2_],
                                  in_=psA[:, s2_]))(pm, sl_), inc="sv")
                else:
                    va = S.op("vector",
                              (lambda ml_, s2_: lambda h: h.tensor_tensor(
                                  out=s_acc[ml_][:, s2_],
                                  in0=s_acc[ml_][:, s2_],
                                  in1=psA[:, s2_], op=OP.add))(pm, sl_),
                              inc="sv")
                psA_free[2 * hf].append(("sv", va))
                psA_free[2 * hf + 1].append(("sv", va))
            pend["e"] = None

        for half in range(2):
            a2_dma["i"] = 0
            a2_dma["toks"] = {}
            a2_issue()
            for tile in range(16):
                a2_prep(half, 0, tile)

            def jq(j):
                return "qt%d" % (j % 3)

            def j_reload(j):
                if j >= 32:
                    return
                ic, mloc = j // 4, j % 4
                m = 4 * half + mloc
                tb = t1r_t[j % 3]
                tb.wait_free(S, "scalar")
                t1_reload["toks"][j] = (jq(j), S.dma(
                    "scalar", jq(j),
                    (lambda b_, m_, ic_: lambda h: h.dma_start(
                        out=b_.rearrange("p (o c) -> p o c", c=512),
                        in_=t1_h.ap()[m_, 2 * ic_:2 * ic_ + 2].rearrange(
                            "o p c -> p o c")))(tb.ap, m, ic)))

            def j_quant(j):
                if j >= 32:
                    return
                m = 4 * half + (j % 4)
                tb = t1r_t[j % 3]
                S.wait_tok("scalar", t1_reload["toks"].pop(j))
                s1 = S.op("scalar",
                          (lambda b_, m_: lambda h: h.activation(
                              out=b_, in_=b_, func=AF.Relu,
                              scale=cm2[:, m_:m_ + 1], bias=mgc))(
                              tb.ap, m), inc="sa")
                qb = hqn2[j % 2]
                qb.wait_free(S, "vector")
                S.wait("vector", "sa", s1)
                v2 = S.op("vector",
                          (lambda b_, q_: lambda h: h.tensor_scalar(
                              out=q_, in0=b_, scalar1=nmgc,
                              scalar2=-128.0, op0=OP.add, op1=OP.max))(
                              tb.ap, qb.ap), inc="sv")
                tb.used("sv", v2)
                hq_pipe[j] = (v2, qb)

            def j_tp(j):
                if j >= 32:
                    return
                v2, qb = hq_pipe[j]
                hqb = hq_fm2[j % 2]

                def d_hq(k0, nk, _b=hqb.ap):
                    return _b.rearrange("p (s c) -> p s c", c=128)[
                        :, k0:k0 + nk, :]

                tok = transpose_group(qb.ap, 8, d_hq, clip=(-128.0, 127.0),
                                      extra_wait=("sv", v2), dest_wait=hqb)
                qb.used("st", S.cnt["st"])
                hq_pipe[j] = (tok, hqb)

            j_reload(0)
            j_reload(1)
            j_reload(2)
            j_quant(0)
            j_quant(1)
            j_tp(0)
            for j in range(32):
                ic, mloc = j // 4, j % 4
                par = ic % 2
                w3 = wq28[par].ap.bitcast(FP8).rearrange(
                    "p (s n) -> p s n", n=2048)
                if mloc == 0:
                    S.wait("tensor", "sv", wq28_ready[par])
                j_reload(j + 3)
                j_quant(j + 2)
                j_tp(j + 1)
                hq_tok, hqb = hq_pipe.pop(j)
                S.wait("tensor", "sv", hq_tok)
                hq3 = hqb.ap.rearrange("p (s c) -> p s c", c=128)
                evq = []
                for oq in range(4):
                    psA_wait([oq])
                    for s_ in range(8):
                        tok_b = S.op(
                            "tensor",
                            (lambda oq_, s2_, st_, sp_, h3_, w3_:
                             lambda h: h.matmul(
                                 out=psA[:, oq_ * 512:(oq_ + 1) * 512],
                                 lhsT=h3_[:, s2_, :],
                                 rhs=w3_[:, s2_, oq_ * 512:(oq_ + 1) * 512],
                                 start=st_, stop=sp_))(
                                oq, s_, s_ == 0, s_ == 7, hq3, w3),
                            inc="st" if s_ == 7 else None)
                    evq.append((oq, tok_b))
                    if len(evq) > 1:
                        oq_, tokb_ = evq.pop(0)
                        S.wait("vector", "st", tokb_)
                        sl_ = slice(oq_ * 512, oq_ * 512 + 512)
                        if ic == 0:
                            va = S.op("vector",
                                      (lambda ml_, s2_: lambda h: h.tensor_copy(
                                          out=s_acc[ml_][:, s2_],
                                          in_=psA[:, s2_]))(mloc, sl_),
                                      inc="sv")
                        else:
                            va = S.op("vector",
                                      (lambda ml_, s2_: lambda h:
                                       h.tensor_tensor(
                                           out=s_acc[ml_][:, s2_],
                                           in0=s_acc[ml_][:, s2_],
                                           in1=psA[:, s2_], op=OP.add))(
                                          mloc, sl_), inc="sv")
                        psA_free[oq_].append(("sv", va))
                oq_, tokb_ = evq.pop(0)
                S.wait("vector", "st", tokb_)
                sl_ = slice(oq_ * 512, oq_ * 512 + 512)
                if ic == 0:
                    va = S.op("vector",
                              (lambda ml_, s2_: lambda h: h.tensor_copy(
                                  out=s_acc[ml_][:, s2_],
                                  in_=psA[:, s2_]))(mloc, sl_), inc="sv")
                else:
                    va = S.op("vector",
                              (lambda ml_, s2_: lambda h: h.tensor_tensor(
                                  out=s_acc[ml_][:, s2_],
                                  in0=s_acc[ml_][:, s2_],
                                  in1=psA[:, s2_], op=OP.add))(mloc, sl_),
                              inc="sv")
                psA_free[oq_].append(("sv", va))
                hqb.used("st", S.cnt["st"])
                if ic + 1 < 8:
                    for tt_ in range(2):
                        a2_prep(half, ic + 1, 4 * mloc + tt_)
                    for tt_ in range(2):
                        a2_prep(half, ic + 1, 4 * mloc + 2 + tt_)
                if mloc == 3:
                    wq28[par].used("st", S.cnt["st"])
                if ic == 7:
                    a2_combine(half, mloc)
        fa.pop()
        ba.pop()
        ua.pop()

        DBG.update(dict(invs1b=invs1b, rs1dbg=rs1dbg, ms1dbg=ms1dbg,
                        nmean=nmean, rstd=rstd, sq1=sq1, mx1=mx1, cm1=cm1,
                        invs1=invs1, c1y=c1y, sq2=sq2, am2=am2, cm2=cm2,
                        c2y=c2y, invsw1=invsw1, invsw2=invsw2, sw1=sw1,
                        sw2=sw2, mgc=mgc, xq_fm=xq_fm, dsc=dsc,
                        wsums=wsums, wsums2=wsums2))
        if KDBG:
            def _dump(col, ap, n=8):
                S.op("vector", (lambda c_, a_, n_: lambda h: h.tensor_copy(
                    out=dbgt[:, c_:c_ + n_], in_=a_))(col, ap, n))
            dbgt = fa.alloc(64)
            for i, (nm_, ap_, n_) in enumerate([
                    ("c2y", c2y, 8), ("cm2", cm2, 8), ("c1y", c1y, 8),
                    ("cm1", cm1, 8), ("rstd", rstd, 8), ("nmean", nmean, 8)]):
                _dump(8 * i, ap_, n_)
            _dump(48, sw1, 1)
            _dump(49, sw2, 1)
            _dump(50, invsw1, 1)
            _dump(51, invsw2, 1)
            dl = S.op("vector", lambda h: h.tensor_copy(
                out=dbgt[:, 52:53], in_=mgc), inc="sv")
            S.wait("scalar", "sv", dl)
            S.dma("scalar", "qo0", lambda h: h.dma_start(out=dbg_h[:],
                                                          in_=dbgt))
            S.dma("scalar", "qo0", lambda h: h.dma_start(
                out=dbg2_h[:, 0:192], in_=wsums))
            S.dma("scalar", "qo0", lambda h: h.dma_start(
                out=dbg2_h[:, 192:384], in_=wsums2))
        for q in DMA_SEMS:
            if S.cnt[q]:
                eng = ("gpsimd" if q.startswith(("qg", "qd")) else
                       "sync" if q in ("qsu", "qx0", "qx1", "qw0", "qw1")
                       else "scalar")
                S.wait(eng, q, S.cnt[q])

        blk = st.enter_context(nc.Block())

        @blk.sync
        def _(h):
            S.emit("sync", h)

        @blk.scalar
        def _(h):
            S.emit("scalar", h)

        @blk.vector
        def _(h):
            S.emit("vector", h)

        @blk.tensor
        def _(h):
            S.emit("tensor", h)

        @blk.gpsimd
        def _(h):
            S.emit("gpsimd", h)

    return nc


DBG = {}


_NC_CACHE = {}


def kernel(L0_attn_x, L0_attn_w_gate, L0_attn_g_gate, L0_attn_w_down,
           L0_attn_g_down, L0_ffn_gate_w, L0_ffn_up_w, L0_ffn_down_w,
           L0_ffn_gate_scale, L0_ffn_up_scale, L0_ffn_down_scale):
    if "nc" not in _NC_CACHE:
        _NC_CACHE["nc"] = build_nc()
    nc = _NC_CACHE["nc"]

    x = np.ascontiguousarray(np.asarray(L0_attn_x, dtype=np.float32)
                             .reshape(8 * T, D))
    common = {
        "wg": np.ascontiguousarray(L0_attn_w_gate, dtype=np.float32),
        "gg": np.ascontiguousarray(L0_attn_g_gate, dtype=np.float32),
        "wd": np.ascontiguousarray(L0_attn_w_down, dtype=np.float32),
        "gd": np.ascontiguousarray(L0_attn_g_down, dtype=np.float32),
        "fgw": np.ascontiguousarray(L0_ffn_gate_w, dtype=np.float32),
        "fuw": np.ascontiguousarray(L0_ffn_up_w, dtype=np.float32),
        "fdw": np.ascontiguousarray(L0_ffn_down_w, dtype=np.float32),
        "fgs": np.ascontiguousarray(L0_ffn_gate_scale, dtype=np.float32),
        "fus": np.ascontiguousarray(L0_ffn_up_scale, dtype=np.float32),
        "fds": np.ascontiguousarray(L0_ffn_down_scale, dtype=np.float32),
        "ident": np.eye(128, dtype=ml_dtypes.bfloat16),
    }
    in_maps = []
    for c in range(8):
        mm = dict(common)
        mm["x"] = np.ascontiguousarray(x[c * T:(c + 1) * T])
        in_maps.append(mm)
    res = run_bass_kernel_spmd(nc, in_maps, core_ids=list(range(8)))
    out = np.concatenate([np.asarray(r["out"]) for r in res.results], axis=0)
    return out.reshape(4, 2048, 2048).astype(np.float32, copy=False)
